# version 20
# baseline (speedup 1.0000x reference)
import numpy as np
import ml_dtypes
import concourse.bass as bass
import concourse.mybir as mybir
from concourse.bass_utils import run_bass_kernel_spmd

F32 = mybir.dt.float32
BF16 = mybir.dt.bfloat16
I32 = mybir.dt.int32
AF = mybir.ActivationFunctionType
ALU = mybir.AluOpType

T = 1024
H = 2048
HC = 16
NSUP = 16
NH = 16
SCALE = 1.0 / np.sqrt(128.0)
CAP = 30.0
EPS = 1e-5
NCORES = 8

A_COLS = 40960
A_W0 = [0, 8192]
A_W1 = [16384, 24576]
A_DT = [32768, 36864]
A_WQ = [0, 2048]
A_WK = [4096, 6144]
A_WV = [8192, 10240]
A_WO = [12288, 14336, 16384, 18432]
A_QOWN = [20480, 21504]
A_QST = [22528, 23552]
A_KST = [24576, 26624]
A_VST = [28672, 30720]
A_TS = [32768, 33792]
A_RB = 34816
A_SUM = 36864

B_COLS = 14336
B_KOWN = [0, 1024]
B_VFM = [2048, 3072]
B_VTM = [4096, 5120]
NATT = 5
B_ATT = [6144 + 1024 * i for i in range(NATT)]
B_P = [11264, 12288, 13312]
B_XB = [0, 1024, 2048, 3072]
B_XSQ = [4096, 5120, 6144, 7168]
B_MEAN = 8192
B_ROW2 = 10240
B_RSTD = 12288


def _i_span(jt):
    lo = max(0, jt - 8)
    hi = min(8, jt + 1)
    return lo, hi


class Ctr:
    def __init__(self):
        self.v = 0

    def bump(self, k=1):
        self.v += k
        return self.v


class Rec:
    def __init__(self):
        self.ops = {k: [] for k in ("tensor", "scalar", "vector", "gpsimd", "sync")}

    def on(self, eng, fn):
        self.ops[eng].append(fn)


def build_nc():
    nc = bass.Bass("TRN2", target_bir_lowering=False, debug=False,
                   dynamic_dma_scratch_size=16384)

    x_d = nc.dram_tensor("x", [H, T], F32, kind="ExternalInput").ap()
    w0t_d = nc.dram_tensor("w0t", [H, 8192], F32, kind="ExternalInput").ap()
    w1t_d = nc.dram_tensor("w1t", [8192, H], F32, kind="ExternalInput").ap()
    wqt_d = nc.dram_tensor("wqt", [H, H], F32, kind="ExternalInput").ap()
    wkt_d = nc.dram_tensor("wkt", [H, H], F32, kind="ExternalInput").ap()
    wvt_d = nc.dram_tensor("wvt", [H, H], F32, kind="ExternalInput").ap()
    wot_d = nc.dram_tensor("wot", [H, H], F32, kind="ExternalInput").ap()
    b0_d = nc.dram_tensor("b0", [128, 64], F32, kind="ExternalInput").ap()
    b1_d = nc.dram_tensor("b1", [128, 16], F32, kind="ExternalInput").ap()
    maskl_d = nc.dram_tensor("maskl", [128, 128], F32, kind="ExternalInput").ap()
    maskd_d = nc.dram_tensor("maskd", [128, 128], F32, kind="ExternalInput").ap()
    maskz_d = nc.dram_tensor("maskz", [128, 128], F32, kind="ExternalInput").ap()
    idx_d = nc.dram_tensor("idx", [1, 1], I32, kind="ExternalInput").ap()
    y_d = nc.dram_tensor("y", [H, T], F32, kind="ExternalOutput").ap()

    wq_bf = nc.dram_tensor("wq_bf", [H, H], BF16)
    wk_bf = nc.dram_tensor("wk_bf", [H, H], BF16)
    wv_bf = nc.dram_tensor("wv_bf", [H, H], BF16)
    wo_bf = nc.dram_tensor("wo_bf", [H, H], BF16)
    kvc = [nc.dram_tensor(f"kvc{i}", [4, 128, 2048], BF16) for i in range(4)]
    kvg = [nc.dram_tensor(f"kvg{i}", [NCORES, 4, 128, 2048], BF16,
                          addr_space="Shared") for i in range(4)]
    qh_d = nc.dram_tensor("qh", [NH, 128, T], BF16)

    ident_h = nc.inline_tensor(np.eye(128, dtype=ml_dtypes.bfloat16), name="ident")
    ones128_h = nc.inline_tensor(np.ones((128, 1), dtype=ml_dtypes.bfloat16),
                                 name="ones128")
    onesrow_h = nc.inline_tensor(np.ones((1, 128), dtype=np.float32),
                                 name="onesrow")
    onesmat_h = nc.inline_tensor(np.ones((128, 128), dtype=ml_dtypes.bfloat16),
                                 name="onesmat")

    x_sb = nc.alloc_sbuf_tensor("x_sb", [128, HC * T], F32).ap()
    xln = nc.alloc_sbuf_tensor("xln", [128, HC * T], BF16).ap()
    A = nc.alloc_sbuf_tensor("arenaA", [128, A_COLS], BF16).ap()
    B = nc.alloc_sbuf_tensor("arenaB", [128, B_COLS], BF16).ap()
    b0_sb = nc.alloc_sbuf_tensor("b0_sb", [128, 64], F32).ap()
    b1_sb = nc.alloc_sbuf_tensor("b1_sb", [128, 16], F32).ap()
    b1h_sb = nc.alloc_sbuf_tensor("b1h_sb", [128, 16], F32).ap()
    maskl_sb = nc.alloc_sbuf_tensor("maskl_sb", [128, 128], BF16).ap()
    maskd_sb = nc.alloc_sbuf_tensor("maskd_sb", [128, 128], BF16).ap()
    maskz_sb = nc.alloc_sbuf_tensor("maskz_sb", [128, 128], BF16).ap()
    ident_sb = nc.alloc_sbuf_tensor("ident_sb", [128, 128], BF16).ap()
    ones_sb = nc.alloc_sbuf_tensor("ones_sb", [128, 1], BF16).ap()
    onesrow_sb = nc.alloc_sbuf_tensor("onesrow_sb", [1, 128], F32).ap()
    onesmat_sb = nc.alloc_sbuf_tensor("onesmat_sb", [128, 128], BF16).ap()
    eps_sb = nc.alloc_sbuf_tensor("eps_sb", [1, 1], F32).ap()
    idx_sb = nc.alloc_sbuf_tensor("idx_sb", [1, 1], I32).ap()
    masks = {"l": maskl_sb, "d": maskd_sb, "z": maskz_sb}

    Bf = B.bitcast(F32)
    Af = A.bitcast(F32)

    def b_f32(col_bf16, n_f32, p0=0, p1=128):
        return Bf[p0:p1, col_bf16 // 2: col_bf16 // 2 + n_f32]

    pA = nc.alloc_psum_tensor("pA", [128, 512], F32).ap()
    pB = nc.alloc_psum_tensor("pB", [128, 512], F32).ap()
    pC = nc.alloc_psum_tensor("pC", [128, 512], F32).ap()
    pD = nc.alloc_psum_tensor("pD", [128, 512], F32).ap()
    psbig = nc.alloc_psum_tensor("psbig", [128, 1024], F32).ap()
    psbig2 = nc.alloc_psum_tensor("psbig2", [128, 1024], F32).ap()
    pCD = [pC, pD]
    pCbf = pC.bitcast(BF16)

    sem_names = ["s_x", "s_xc", "s_w", "s_mm1", "s_gel", "s_mm2", "s_add",
                 "s_lnxb", "s_lnsq", "s_lnmm", "s_lnrow", "s_lnbc", "s_lnap",
                 "s_aw", "s_pe", "s_kvout", "s_cc", "s_win", "s_st", "s_tanh",
                 "s_exp", "s_sum", "s_pv", "s_rcp", "s_rb", "s_av", "s_oproj",
                 "s_oadd", "s_out", "s_init", "s_qw", "s_pc", "s_scp", "s_avc", "s_lnm"]
    sems, ctrs, sem_guards = {}, {}, []
    for n in sem_names:
        g = nc.semaphore(n)
        sems[n] = g.__enter__()
        sem_guards.append(g)
        ctrs[n] = Ctr()

    rec = Rec()

    def S(n):
        return sems[n]

    def C(n):
        return ctrs[n]

    def _sync_init(e):
        xr = x_d.rearrange("(c p) t -> p c t", p=128)
        for hc in range(HC):
            e.dma_start(out=x_sb[:, hc * T:(hc + 1) * T],
                        in_=xr[:, hc, :]).then_inc(S("s_xc"), 16)
        e.dma_start(out=b0_sb[:, :], in_=b0_d[:, :]).then_inc(S("s_x"), 16)
        e.dma_start(out=b1_sb[:, :], in_=b1_d[:, :]).then_inc(S("s_x"), 16)
        e.dma_start(out=ident_sb[:, :], in_=ident_h.ap()[:, :]).then_inc(S("s_x"), 16)
        e.dma_start(out=ones_sb[:, :], in_=ones128_h.ap()[:, :]).then_inc(S("s_x"), 16)
        e.dma_start(out=onesrow_sb[:, :], in_=onesrow_h.ap()[:, :]).then_inc(S("s_x"), 16)
        e.dma_start(out=onesmat_sb[:, :], in_=onesmat_h.ap()[:, :]).then_inc(S("s_x"), 16)
        e.dma_start(out=idx_sb[:, :], in_=idx_d[:, :]).then_inc(S("s_x"), 16)
    rec.on("sync", _sync_init)
    C("s_xc").bump(16 * HC)
    C("s_x").bump(16 * 7)

    def _gp_init(e):
        e.dma_start(out=masks["l"][:, :], in_=maskl_d[:, :]).then_inc(S("s_x"), 16)
        e.dma_start(out=masks["d"][:, :], in_=maskd_d[:, :]).then_inc(S("s_x"), 16)
        e.dma_start(out=masks["z"][:, :], in_=maskz_d[:, :]).then_inc(S("s_x"), 16)
    rec.on("gpsimd", _gp_init)
    C("s_x").bump(16 * 3)
    x_loaded = C("s_x").v

    def _vec_init(e):
        e.wait_ge(S("s_x"), x_loaded)
        e.memset(eps_sb[:, :], EPS)
        e.tensor_scalar_mul(b1h_sb[:, :], b1_sb[:, :], 0.5).then_inc(S("s_init"), 1)
    rec.on("vector", _vec_init)
    C("s_init").bump(1)
    init_done = C("s_init").v

    def emit_ln(ready_sem, ready_val, chunk_gate=None, pre_info=None):
        xb = [B[:, B_XB[p]:B_XB[p] + T] for p in range(4)]
        xsq = [B[:, B_XSQ[p]:B_XSQ[p] + T] for p in range(4)]
        mean_r = b_f32(B_MEAN, T, 0, 1)
        row2 = b_f32(B_ROW2, T, 0, 1)
        rstd_r = b_f32(B_RSTD, T, 0, 1)

        if pre_info is not None:
            lnmm_base = pre_info["lnmm"]
            lnsq_base = pre_info["lnsq"]
        else:
            lnmm_base = C("s_lnmm").v
            lnsq_base = C("s_lnsq").v
        lnxb_base = C("s_lnxb").v

        def _vec_cast(e):
            for hc in range(HC):
                if chunk_gate is not None:
                    gs, gv = chunk_gate(hc)
                    e.wait_ge(S(gs), gv)
                elif hc == 0:
                    e.wait_ge(S(ready_sem), ready_val)
                if hc >= 2:
                    e.wait_ge(S("s_lnmm"), lnmm_base + hc - 1)
                e.tensor_copy(xb[hc % 2][:, :],
                              x_sb[:, hc * T:(hc + 1) * T])
                e.tensor_mul(xsq[hc % 2][:, :], xb[hc % 2][:, :],
                             xb[hc % 2][:, :]).then_inc(S("s_lnsq"), 1)
        if pre_info is None:
            rec.on("vector", _vec_cast)
            C("s_lnsq").bump(HC)

        def _ten_stats(e):
            for hc in range(HC):
                if hc == 0:
                    e.wait_ge(S("s_x"), x_loaded)
                    if chunk_gate is not None:
                        e.wait_ge(S(ready_sem), ready_val)
                e.wait_ge(S("s_lnsq"), lnsq_base + hc + 1)
                for ih in range(2):
                    e.matmul(psbig[0:1, ih * 512:(ih + 1) * 512], ones_sb[:, :],
                             xb[hc % 2][:, ih * 512:(ih + 1) * 512],
                             start=(hc == 0), stop=(hc == HC - 1))
                ins = None
                for ih in range(2):
                    ins = e.matmul(psbig2[0:1, ih * 512:(ih + 1) * 512],
                                   ones_sb[:, :],
                                   xsq[hc % 2][:, ih * 512:(ih + 1) * 512],
                                   start=(hc == 0), stop=(hc == HC - 1))
                ins.then_inc(S("s_lnmm"), 1)
        if pre_info is None:
            rec.on("tensor", _ten_stats)
            C("s_lnmm").bump(HC)
        stats_done = lnmm_base + HC

        def _vec_rows(e):
            e.wait_ge(S("s_lnmm"), stats_done)
            e.tensor_scalar_mul(mean_r, psbig[0:1, :], 1.0 / H).then_inc(S("s_lnm"), 1)
            e.tensor_scalar_mul(row2, psbig2[0:1, :], 1.0 / H)
            e.tensor_mul(rstd_r, mean_r, mean_r)
            e.tensor_sub(row2, row2, rstd_r).then_inc(S("s_lnrow"), 1)
        rec.on("vector", _vec_rows)
        C("s_lnm").bump(1)
        mean_done = C("s_lnm").v
        C("s_lnrow").bump(1)
        var_done = C("s_lnrow").v

        def _ten_bcM(e):
            e.wait_ge(S("s_lnm"), mean_done)
            ins = None
            for ih in range(2):
                ins = e.matmul(psbig[:, ih * 512:(ih + 1) * 512], onesrow_sb[:, :],
                               mean_r[:, ih * 512:(ih + 1) * 512],
                               start=True, stop=True)
            ins.then_inc(S("s_lnbc"), 1)
        rec.on("tensor", _ten_bcM)
        C("s_lnbc").bump(1)
        bcm_done = C("s_lnbc").v

        def _vec_pass1(e):
            e.wait_ge(S("s_lnbc"), bcm_done)
            for hc in range(HC):
                e.tensor_sub(xln[:, hc * T:(hc + 1) * T],
                             x_sb[:, hc * T:(hc + 1) * T], psbig[:, :])
            e.engine_nop().then_inc(S("s_lnrow"), 1)
        rec.on("vector", _vec_pass1)
        C("s_lnrow").bump(1)

        def _sca_sqrt(e):
            e.wait_ge(S("s_lnrow"), var_done)
            e.activation(out=row2, in_=row2, func=AF.Sqrt,
                         bias=eps_sb[0:1, :]).then_inc(S("s_lnrow"), 1)
        rec.on("scalar", _sca_sqrt)
        C("s_lnrow").bump(1)
        sqrt_done = C("s_lnrow").v

        def _vec_recip(e):
            e.wait_ge(S("s_lnrow"), sqrt_done)
            e.reciprocal(rstd_r, row2).then_inc(S("s_lnrow"), 1)
        rec.on("vector", _vec_recip)
        C("s_lnrow").bump(1)
        rows_done = C("s_lnrow").v

        def _ten_bcR(e):
            e.wait_ge(S("s_lnrow"), rows_done)
            ins = None
            for ih in range(2):
                ins = e.matmul(psbig2[:, ih * 512:(ih + 1) * 512], onesrow_sb[:, :],
                               rstd_r[:, ih * 512:(ih + 1) * 512],
                               start=True, stop=True)
            ins.then_inc(S("s_lnbc"), 1)
        rec.on("tensor", _ten_bcR)
        C("s_lnbc").bump(1)
        bc_done = C("s_lnbc").v

        def _vec_apply(e):
            e.wait_ge(S("s_lnbc"), bc_done)
            for hc in range(HC):
                e.tensor_mul(xln[:, hc * T:(hc + 1) * T],
                             xln[:, hc * T:(hc + 1) * T],
                             psbig2[:, :]).then_inc(S("s_lnap"), 1)
        rec.on("vector", _vec_apply)
        C("s_lnap").bump(HC)
        return C("s_lnap").v

    def emit_ffn(ln_done, region_ready, gp_hook=None, early_gates=None,
                 ln_hooks=None):
        w0 = [A[:, A_W0[p]:A_W0[p] + 8192] for p in range(2)]
        w1 = [A[:, A_W1[p]:A_W1[p] + 8192] for p in range(2)]
        dt = [A[:, A_DT[p]:A_DT[p] + 4096] for p in range(2)]

        w_base = C("s_w").v
        mm1_base = C("s_mm1").v
        gel_base = C("s_gel").v
        mm2_base = C("s_mm2").v
        add_base = C("s_add").v

        def _vec_b1(e):
            e.wait_ge(S("s_init"), init_done)
            for hc in range(HC):
                e.tensor_scalar_add(x_sb[:, hc * T:(hc + 1) * T],
                                    x_sb[:, hc * T:(hc + 1) * T],
                                    b1h_sb[:, hc:hc + 1])
            e.engine_nop().then_inc(S("s_add"), 1)
        rec.on("vector", _vec_b1)
        C("s_add").bump(1)

        def _gp_w(e):
            if region_ready is not None:
                e.wait_ge(S(region_ready[0]), region_ready[1])
            for s in range(NSUP):
                if gp_hook is not None:
                    gp_hook(e, s)
                if s >= 2:
                    e.wait_ge(S("s_mm2"), mm2_base + 32 * (s - 1))
                if early_gates is not None and s in early_gates:
                    for gsem, gval in early_gates[s]:
                        e.wait_ge(S(gsem), gval)
                p = s % 2
                e.dma_start(
                    out=w0[p].rearrange("p (c n) -> p c n", n=512),
                    in_=w0t_d.rearrange("(c p) n -> p c n", p=128)[:, :, s * 512:(s + 1) * 512],
                ).then_inc(S("s_w"), 16)
                e.dma_start(
                    out=w1[p].rearrange("p (c h) -> p c h", h=H),
                    in_=w1t_d.rearrange("(S c p) h -> S p c h", p=128, c=4)[s],
                ).then_inc(S("s_w"), 16)
        rec.on("gpsimd", _gp_w)
        C("s_w").bump(32 * NSUP)

        def _ten(e):
            for s in range(NSUP):
                p = s % 2
                for nc4 in range(4):
                    for ih in range(2):
                        tile = s * 8 + nc4 * 2 + ih
                        ps = pA if tile % 2 == 0 else pB
                        if nc4 == 0 and ih == 0:
                            e.wait_ge(S("s_w"), w_base + 32 * (s + 1))
                        if tile >= 2:
                            e.wait_ge(S("s_gel"), gel_base + tile - 1)
                        ins = None
                        for hc in range(HC):
                            if s == 0 and nc4 == 0 and ih == 0:
                                e.wait_ge(S("s_lnap"), ln_done - HC + hc + 1)
                            ins = e.matmul(
                                ps[:, :],
                                w0[p][:, hc * 512 + nc4 * 128:hc * 512 + (nc4 + 1) * 128],
                                xln[:, hc * T + ih * 512:hc * T + (ih + 1) * 512],
                                start=(hc == 0), stop=(hc == HC - 1))
                        ins.then_inc(S("s_mm1"), 1)
                last = (s == NSUP - 1) and (ln_hooks is not None)
                for hcc in range(HC):
                    for ih2 in range(2):
                        et = s * 32 + hcc * 2 + ih2
                        if last:
                            psE = pCD[ih2]
                            prev_et = et - 4 if et < s * 32 + 2 else et - 2
                            e.wait_ge(S("s_add"), add_base + prev_et + 2)
                        else:
                            rot = et % 4
                            psE = (pC, pD, psbig[:, 0:512], psbig[:, 512:1024])[rot]
                            if et >= 4:
                                e.wait_ge(S("s_add"), add_base + et - 2)
                        if hcc == 0 and ih2 == 0:
                            e.wait_ge(S("s_gel"), gel_base + 8 * (s + 1))
                        ins = None
                        for nc4 in range(4):
                            ins = e.matmul(
                                psE,
                                w1[p][:, nc4 * 2048 + hcc * 128:nc4 * 2048 + (hcc + 1) * 128],
                                dt[p][:, nc4 * 1024 + ih2 * 512:nc4 * 1024 + (ih2 + 1) * 512],
                                start=(nc4 == 0), stop=(nc4 == 3))
                        ins.then_inc(S("s_mm2"), 1)
                        if last and ih2 == 1:
                            ln_hooks["ten"](e, hcc)
        rec.on("tensor", _ten)
        C("s_mm1").bump(8 * NSUP)
        C("s_mm2").bump(32 * NSUP)

        def _sca_gelu(e):
            for s in range(NSUP):
                p = s % 2
                for nc4 in range(4):
                    for ih in range(2):
                        tile = s * 8 + nc4 * 2 + ih
                        ps = pA if tile % 2 == 0 else pB
                        if s >= 2 and nc4 == 0 and ih == 0:
                            e.wait_ge(S("s_mm2"), mm2_base + 32 * (s - 1))
                        e.wait_ge(S("s_mm1"), mm1_base + tile + 1)
                        e.activation(
                            out=dt[p][:, nc4 * 1024 + ih * 512:nc4 * 1024 + (ih + 1) * 512],
                            in_=ps[:, :], func=AF.Gelu,
                            bias=b0_sb[:, s * 4 + nc4:s * 4 + nc4 + 1],
                        ).then_inc(S("s_gel"), 1)
        rec.on("scalar", _sca_gelu)
        C("s_gel").bump(8 * NSUP)

        def _vec_add(e):
            for s in range(NSUP):
                for hcc in range(HC):
                    for ih2 in range(2):
                        et = s * 32 + hcc * 2 + ih2
                        if (s == NSUP - 1) and (ln_hooks is not None):
                            psE = pCD[ih2]
                        else:
                            rot = et % 4
                            psE = (pC, pD, psbig[:, 0:512], psbig[:, 512:1024])[rot]
                        e.wait_ge(S("s_mm2"), mm2_base + et + 1)
                        xs = x_sb[:, hcc * T + ih2 * 512:hcc * T + (ih2 + 1) * 512]
                        e.scalar_tensor_tensor(
                            out=xs, in0=psE, scalar=0.5, in1=xs,
                            op0=ALU.mult, op1=ALU.add,
                        ).then_inc(S("s_add"), 1)
                        if (s == NSUP - 1) and (ln_hooks is not None) and ih2 == 1:
                            ln_hooks["vec"](e, hcc)
        rec.on("vector", _vec_add)
        C("s_add").bump(32 * NSUP)
        return C("s_add").v

    def make_ln_pre():
        xb = [B[:, B_XB[p]:B_XB[p] + T] for p in range(2)]
        xsq = [B[:, B_XSQ[p]:B_XSQ[p] + T] for p in range(2)]
        info = {"lnsq": C("s_lnsq").v, "lnmm": C("s_lnmm").v}

        def vec_hook(e, hc):
            e.tensor_copy(xb[hc % 2][:, :], x_sb[:, hc * T:(hc + 1) * T])
            e.tensor_mul(xsq[hc % 2][:, :], xb[hc % 2][:, :],
                         xb[hc % 2][:, :]).then_inc(S("s_lnsq"), 1)

        def ten_hook(e, hc):
            e.wait_ge(S("s_lnsq"), info["lnsq"] + hc + 1)
            for ih in range(2):
                e.matmul(psbig[0:1, ih * 512:(ih + 1) * 512], ones_sb[:, :],
                         xb[hc % 2][:, ih * 512:(ih + 1) * 512],
                         start=(hc == 0), stop=(hc == HC - 1))
            ins = None
            for ih in range(2):
                ins = e.matmul(psbig2[0:1, ih * 512:(ih + 1) * 512],
                               ones_sb[:, :],
                               xsq[hc % 2][:, ih * 512:(ih + 1) * 512],
                               start=(hc == 0), stop=(hc == HC - 1))
            ins.then_inc(S("s_lnmm"), 1)

        C("s_lnsq").bump(HC)
        C("s_lnmm").bump(HC)
        return {"ten": ten_hook, "vec": vec_hook}, info

    def emit_attn(ln_done, ffn1_add_done, ln_hooks=None):
        wq = [A[:, A_WQ[p]:A_WQ[p] + 2048] for p in range(2)]
        wk = [A[:, A_WK[p]:A_WK[p] + 2048] for p in range(2)]
        wv = [A[:, A_WV[p]:A_WV[p] + 2048] for p in range(2)]
        wo = [A[:, A_WO[p]:A_WO[p] + 2048] for p in range(4)]
        qown = [A[:, A_QOWN[p]:A_QOWN[p] + 1024] for p in range(2)]
        qst = [A[:, A_QST[p]:A_QST[p] + 1024] for p in range(2)]
        kst = [A[:, A_KST[p]:A_KST[p] + 2048] for p in range(2)]
        vst = [A[:, A_VST[p]:A_VST[p] + 2048] for p in range(2)]
        ts = [Af[:, A_TS[p] // 2: A_TS[p] // 2 + 512] for p in range(2)]
        kown = [B[:, B_KOWN[p]:B_KOWN[p] + 1024] for p in range(2)]
        vfm = [B[:, B_VFM[p]:B_VFM[p] + 1024] for p in range(2)]
        vtm = [B[:, B_VTM[p]:B_VTM[p] + 1024] for p in range(2)]
        att = [B[:, B_ATT[i]:B_ATT[i] + 1024] for i in range(NATT)]
        psl = [B[:, B_P[p]:B_P[p] + 1024] for p in range(3)]
        rb = Af[:, A_RB // 2: A_RB // 2 + 1024]
        sum_r = Af[0:1, A_SUM // 2: A_SUM // 2 + 1024]

        aw_base = C("s_aw").v
        qw_base = C("s_qw").v
        pe_base = C("s_pe").v
        st_base = C("s_st").v
        kvout_base = C("s_kvout").v
        cc_base = C("s_cc").v
        win_base = C("s_win").v
        tanh_base = C("s_tanh").v
        exp_base = C("s_exp").v
        sum_base = C("s_sum").v
        pv_base = C("s_pv").v
        rcp_base = C("s_rcp").v
        scp_base = C("s_scp").v
        avc_base = C("s_avc").v
        rb_base = C("s_rb").v
        av_base = C("s_av").v
        oproj_base = C("s_oproj").v
        oadd_base = C("s_oadd").v

        def pe_val(h, k):
            return pe_base + 7 * h + k + 1

        def st_val(h, k):
            return st_base + 7 * h + k + 1

        wqt_r = wq_bf.ap().rearrange("(c p) (G o) -> G p c o", p=128, o=128)
        wkt_r = wk_bf.ap().rearrange("(c p) (G o) -> G p c o", p=128, o=128)
        wvt_r = wv_bf.ap().rearrange("(c p) (G o) -> G p c o", p=128, o=128)
        wot_r = wo_bf.ap().rearrange("(G p) g -> G p g", p=128)

        def head_subs():
            half0 = [(jt, 0, max(0, jt - 8), min(4, jt + 1)) for jt in range(0, 12)]
            half1 = [(jt, 1, max(4, jt - 8), min(8, jt + 1)) for jt in range(4, 16)]
            out = []
            for a, b in zip(half0, half1):
                out.append(a)
                out.append(b)
            return out

        subs = []
        for h in range(NH):
            for (jt, half, l2, h2) in head_subs():
                subs.append((h, jt, half, l2, h2, jt == (0 if half == 0 else 4)))
        NSUBH = len(subs) // NH
        prev_parity = {}
        last_par = {}
        for si, (h, jt, half, l2, h2, stf) in enumerate(subs):
            key = (jt % 3, half)
            prev_parity[si] = last_par.get(key)
            last_par[key] = si

        def mask_ops(jt, l2, h2):
            out = []
            for ic in range(l2, h2):
                rt = jt - ic
                if rt == 0:
                    out.append((ic, "l"))
                elif rt == 8:
                    out.append((ic, "d"))
                elif jt < 8:
                    out.append((ic, "z"))
            return out

        def _sync_attn1(e):
            def slices(h):
                p = h % 2
                if h >= 2:
                    e.wait_ge(S("s_st"), st_val(h - 2, 6))
                e.dma_start(out=wq[p].rearrange("p (c o) -> p c o", o=128),
                            in_=wqt_r[h]).then_inc(S("s_aw"), 16)
                e.dma_start(out=wk[p].rearrange("p (c o) -> p c o", o=128),
                            in_=wkt_r[h]).then_inc(S("s_aw"), 16)
                e.dma_start(out=wv[p].rearrange("p (c o) -> p c o", o=128),
                            in_=wvt_r[h]).then_inc(S("s_aw"), 16)

            e.wait_ge(S("s_add"), ffn1_add_done)
            e.wait_ge(S("s_pc"), 256)
            e.wait_ge(S("s_x"), x_loaded)
            prev = e.value_load(idx_sb[0:1, 0:1])

            def kv(h):
                p = h % 2
                g, hh = divmod(h, 4)
                e.wait_ge(S("s_pe"), pe_val(h, 3))
                e.dma_start(out=qh_d.ap()[h], in_=qown[p][:, :]).then_inc(S("s_kvout"), 16)
                e.dma_start(out=kvc[g].ap()[hh, :, 0:1024],
                            in_=kown[p][:, :]).then_inc(S("s_kvout"), 16)
                e.wait_ge(S("s_pe"), pe_val(h, 6))
                e.dma_start(out=kvc[g].ap()[hh, :, 1024:2048],
                            in_=vtm[p][:, :]).then_inc(S("s_kvout"), 16)

            def wowin(h):
                p = h % 2
                g, hh = divmod(h, 4)
                if h >= 4:
                    e.wait_ge(S("s_oproj"),
                              oproj_base + 32 * ((h - 4) // 4 + 1))
                e.dma_start(out=wo[h % 4][:, :],
                            in_=wot_r[h]).then_inc(S("s_qw"), 16)
                e.wait_ge(S("s_cc"), cc_base + g + 1)
                if h >= 2:
                    e.wait_ge(S("s_pv"), pv_base + NSUBH * (h - 1))
                kv_halo = kvg[g].ap()[bass.ts(prev, 1)].squeeze(0)
                e.dma_start(out=kst[p][:, 0:1024],
                            in_=kv_halo[hh, :, 0:1024]).then_inc(S("s_win"), 16)
                e.dma_start(out=vst[p][:, 0:1024],
                            in_=kv_halo[hh, :, 1024:2048]).then_inc(S("s_win"), 16)
                e.dma_start(out=kst[p][:, 1024:2048],
                            in_=kvc[g].ap()[hh, :, 0:1024]).then_inc(S("s_win"), 16)
                e.dma_start(out=vst[p][:, 1024:2048],
                            in_=kvc[g].ap()[hh, :, 1024:2048]).then_inc(S("s_win"), 16)
                e.dma_start(out=qst[p][:, :],
                            in_=qh_d.ap()[h]).then_inc(S("s_win"), 16)

            slices(0)
            slices(1)
            for h in range(NH):
                if h + 2 < NH:
                    slices(h + 2)
                kv(h)
            for h in range(NH):
                wowin(h)
        rec.on("sync", _sync_attn1)
        C("s_aw").bump(48 * NH)
        C("s_kvout").bump(48 * NH)

        def _gp_cc(e):
            for g in range(4):
                e.wait_ge(S("s_kvout"), kvout_base + 48 * 4 * (g + 1))
                e.collective_compute(
                    "AllGather", ALU.bypass,
                    replica_groups=[list(range(NCORES))],
                    ins=[kvc[g].ap().opt()],
                    outs=[kvg[g].ap().opt()],
                ).then_inc(S("s_cc"), 1)
        rec.on("gpsimd", _gp_cc)
        C("s_cc").bump(4)

        C("s_qw").bump(16 * NH)
        C("s_win").bump(80 * NH)

        def _ten_proj(e):
            for h in range(NH):
                p = h % 2
                e.wait_ge(S("s_aw"), aw_base + 48 * (h + 1))
                for wi, wsl in ((0, wq[p]), (1, wk[p]), (2, wv[p])):
                    for th in range(2):
                        t6 = h * 6 + wi * 2 + th
                        ps = pCD[t6 % 2]
                        if t6 >= 2:
                            pt = t6 - 2
                            e.wait_ge(S("s_pe"), pe_val(pt // 6, pt % 6))
                        if wi == 0 and th == 0 and h >= 1:
                            e.wait_ge(S("s_pe"), pe_val(h - 1, 6))
                        ins = None
                        for hc in range(HC):
                            if t6 == 0:
                                e.wait_ge(S("s_lnap"), ln_done - HC + hc + 1)
                            ins = e.matmul(
                                ps[:, :],
                                wsl[:, hc * 128:(hc + 1) * 128],
                                xln[:, hc * T + th * 512:hc * T + (th + 1) * 512],
                                start=(hc == 0), stop=(hc == HC - 1))
                        ins.then_inc(S("s_st"), 1)
                e.wait_ge(S("s_pe"), pe_val(h, 5))
                ins = None
                for tc in range(8):
                    ins = e.transpose(pCbf[:, tc * 128:(tc + 1) * 128],
                                      vfm[p][:, tc * 128:(tc + 1) * 128],
                                      ident_sb[:, :])
                ins.then_inc(S("s_st"), 1)
        rec.on("tensor", _ten_proj)
        C("s_st").bump(7 * NH)

        def _sca_pe(e):
            for h in range(NH):
                p = h % 2
                for wi, dst in ((0, qown[p]), (1, kown[p]), (2, vfm[p])):
                    for th in range(2):
                        t6 = h * 6 + wi * 2 + th
                        e.wait_ge(S("s_st"), st_val(h, wi * 2 + th))
                        if wi == 0 and th == 0 and h >= 2:
                            e.wait_ge(S("s_kvout"), kvout_base + 48 * (h - 1))
                        e.copy(out=dst[:, th * 512:(th + 1) * 512],
                               in_=pCD[t6 % 2][:, :]).then_inc(S("s_pe"), 1)
        rec.on("scalar", _sca_pe)
        C("s_pe").bump(6 * NH)

        def _vec_vtm(e):
            for h in range(NH):
                p = h % 2
                e.wait_ge(S("s_st"), st_val(h, 6))
                if h >= 2:
                    e.wait_ge(S("s_kvout"), kvout_base + 48 * (h - 1))
                e.tensor_copy(vtm[p][:, :], pCbf[:, :]).then_inc(S("s_pe"), 1)
        rec.on("vector", _vec_vtm)
        C("s_pe").bump(NH)

        def _ten_scores(e):
            def do_st(si):
                h, jt, half, l2, h2, stf = subs[si]
                p = h % 2
                ps = pA if si % 2 == 0 else pB
                span = (h2 - l2) * 128
                k = si % NSUBH
                if k == 0:
                    e.wait_ge(S("s_win"), win_base + 80 * (h + 1))
                if si >= 2:
                    e.wait_ge(S("s_tanh"), tanh_base + si - 1)
                mo = mask_ops(jt, l2, h2)
                ins = e.matmul(ps[:, 0:span], kst[p][:, jt * 128:(jt + 1) * 128],
                               qst[p][:, l2 * 128:h2 * 128],
                               start=True, stop=(not mo))
                for mi, (ic, mk) in enumerate(mo):
                    r0 = (ic - l2) * 128
                    ins = e.matmul(ps[:, r0:r0 + 128], ident_sb[:, :],
                                   masks[mk][:, :], start=False,
                                   stop=(mi == len(mo) - 1))
                ins.then_inc(S("s_st"), 1)

            def do_sumpv(si):
                h, jt, half, l2, h2, stf = subs[si]
                p = h % 2
                k = si % NSUBH
                e.wait_ge(S("s_exp"), exp_base + si + 1)
                if k == 0 and h > 0:
                    e.wait_ge(S("s_rcp"), rcp_base + h)
                e.matmul(psbig2[:, l2 * 128:h2 * 128], onesmat_sb[:, :],
                         psl[jt % 3][:, l2 * 128:h2 * 128],
                         start=stf, stop=False).then_inc(S("s_sum"), 1)
                if k == 0 and h > 0:
                    e.wait_ge(S("s_avc"), avc_base + h)
                e.matmul(psbig[:, l2 * 128:h2 * 128],
                         vst[p][:, jt * 128:(jt + 1) * 128],
                         psl[jt % 3][:, l2 * 128:h2 * 128],
                         start=stf, stop=False).then_inc(S("s_pv"), 1)

            def do_oproj(g):
                e.wait_ge(S("s_av"), av_base + 4 * (g + 1))
                e.wait_ge(S("s_qw"), qw_base + 64 * (g + 1))
                if g == 0:
                    e.wait_ge(S("s_pe"), pe_val(NH - 1, 6))
                for g2 in range(HC):
                    for th in range(2):
                        ot = g * 32 + g2 * 2 + th
                        pso = pCD[ot % 2]
                        if ot >= 2:
                            e.wait_ge(S("s_oadd"), oadd_base + ot - 1)
                        elif g > 0:
                            e.wait_ge(S("s_oadd"), oadd_base + 32 * g)
                        ins = None
                        for i in range(4):
                            hh = 4 * g + i
                            ins = e.matmul(
                                pso[:, :],
                                wo[hh % 4][:, g2 * 128:(g2 + 1) * 128],
                                att[hh % NATT][:, th * 512:(th + 1) * 512],
                                start=(i == 0), stop=(i == 3))
                        ins.then_inc(S("s_oproj"), 1)
                        if g == 3 and th == 1 and ln_hooks is not None:
                            ln_hooks["ten"](e, g2)

            DELAY = 3
            for h in range(NH):
                for k in range(NSUBH):
                    si = h * NSUBH + k
                    do_st(si)
                    if k >= DELAY:
                        do_sumpv(si - DELAY)
                for k in range(NSUBH - DELAY, NSUBH):
                    do_sumpv(h * NSUBH + k)
                if h % 4 == 3:
                    do_oproj(h // 4)
        rec.on("tensor", _ten_scores)
        C("s_st").bump(len(subs))
        C("s_sum").bump(len(subs))
        C("s_pv").bump(len(subs))
        C("s_oproj").bump(128)

        def _sca_scores(e):
            for si, (h, jt, half, l2, h2, stf) in enumerate(subs):
                pq = si % 2
                ps = pA if pq == 0 else pB
                span = (h2 - l2) * 128
                e.wait_ge(S("s_st"), st_base + 7 * NH + si + 1)
                e.activation(out=ts[pq][:, 0:span], in_=ps[:, 0:span],
                             func=AF.Tanh, scale=float(SCALE / CAP)
                             ).then_inc(S("s_tanh"), 1)
                pp = prev_parity[si]
                if pp is not None:
                    e.wait_ge(S("s_pv"), pv_base + pp + 1)
                e.activation(out=psl[jt % 3][:, l2 * 128:h2 * 128],
                             in_=ts[pq][:, 0:span],
                             func=AF.Exp, scale=float(CAP)).then_inc(S("s_exp"), 1)
                if si % NSUBH == NSUBH - 1:
                    e.wait_ge(S("s_pv"), pv_base + NSUBH * (h + 1))
                    if h >= NATT:
                        e.wait_ge(S("s_oproj"),
                                  oproj_base + 32 * ((h - NATT) // 4 + 1))
                    e.copy(out=att[h % NATT][:, :],
                           in_=psbig[:, :]).then_inc(S("s_avc"), 1)
        rec.on("scalar", _sca_scores)
        C("s_tanh").bump(len(subs))
        C("s_exp").bump(len(subs))
        C("s_avc").bump(NH)

        def _vec_scores(e):
            for si, (h, jt, half, l2, h2, stf) in enumerate(subs):
                if si % NSUBH == NSUBH - 1:
                    e.wait_ge(S("s_sum"), sum_base + NSUBH * (h + 1))
                    e.tensor_copy(rb[:, 0:512], psbig2[:, 0:512])
                    e.tensor_copy(rb[:, 512:1024],
                                  psbig2[:, 512:1024]).then_inc(S("s_rcp"), 1)
                    e.reciprocal(rb, rb)
                    e.wait_ge(S("s_avc"), avc_base + h + 1)
                    e.scalar_tensor_tensor(
                        out=att[h % NATT][:, :], in0=att[h % NATT][:, :],
                        scalar=1.0, in1=rb, op0=ALU.mult, op1=ALU.mult,
                    ).then_inc(S("s_av"), 1)
                    if h % 4 == 3:
                        g = h // 4
                        for g2 in range(HC):
                            for th in range(2):
                                ot = g * 32 + g2 * 2 + th
                                pso = pCD[ot % 2]
                                e.wait_ge(S("s_oproj"), oproj_base + ot + 1)
                                xs = x_sb[:, g2 * T + th * 512:g2 * T + (th + 1) * 512]
                                e.tensor_add(xs, xs, pso[:, :]).then_inc(S("s_oadd"), 1)
                                if g == 3 and th == 1 and ln_hooks is not None:
                                    ln_hooks["vec"](e, g2)
        rec.on("vector", _vec_scores)
        C("s_rcp").bump(NH)
        C("s_av").bump(NH)
        C("s_oadd").bump(128)
        return {
            "oadd": C("s_oadd").v,
            "proj_st": st_base + 7 * NH,
            "oproj": oproj_base + 128,
            "kvout": kvout_base + 48 * NH,
            "win": win_base + 80 * NH,
            "pv": pv_base + NSUBH * NH,
        }

    ln1_done = emit_ln("s_xc", 16 * HC,
                       chunk_gate=lambda hc: ("s_xc", 16 * (hc + 1)))
    _pc_pairs = [(dst, src) for dst, src in
                 ((wq_bf, wqt_d), (wk_bf, wkt_d), (wv_bf, wvt_d), (wo_bf, wot_d))]

    def _precast(e, sidx):
        for k in range(2):
            i = (sidx - 2) * 2 + k
            if 0 <= i < 16:
                dst, src = _pc_pairs[i // 4]
                r0 = (i % 4) * 512
                e.dma_start(out=dst.ap()[r0:r0 + 512, :],
                            in_=src[r0:r0 + 512, :]).then_inc(S("s_pc"), 16)
    C("s_pc").bump(256)

    ln2h, ln2i = make_ln_pre()
    ffn1_done = emit_ffn(ln1_done, None, gp_hook=_precast, ln_hooks=ln2h)
    ln2_done = emit_ln("s_add", ffn1_done, pre_info=ln2i,
                       chunk_gate=lambda hc: ("s_add", ffn1_done - 30 + 2 * hc))
    ln3h, ln3i = make_ln_pre()
    ainfo = emit_attn(ln2_done, ffn1_done, ln_hooks=ln3h)
    attn_done = ainfo["oadd"]
    ln3_done = emit_ln("s_oadd", attn_done, pre_info=ln3i,
                       chunk_gate=lambda hc: ("s_oadd", attn_done - 30 + 2 * hc))
    ffn2_done = emit_ffn(ln3_done, None, early_gates={
        0: [("s_st", ainfo["proj_st"]), ("s_oproj", ainfo["oproj"]),
            ("s_kvout", ainfo["kvout"]), ("s_pv", ainfo["pv"])],
        1: [("s_pv", ainfo["pv"]), ("s_win", ainfo["win"]),
            ("s_oproj", ainfo["oproj"])],
    })

    def _sync_out(e):
        e.wait_ge(S("s_add"), ffn2_done)
        e.dma_start(out=y_d.rearrange("(c p) t -> p c t", p=128),
                    in_=x_sb.rearrange("p (c t) -> p c t", t=T)
                    ).then_inc(S("s_out"), 16)
        e.wait_ge(S("s_out"), 16)
    rec.on("sync", _sync_out)
    C("s_out").bump(16)

    with nc.Block() as block:
        @block.tensor
        def _(eng):
            for fn in rec.ops["tensor"]:
                fn(eng)

        @block.scalar
        def _(eng):
            for fn in rec.ops["scalar"]:
                fn(eng)

        @block.vector
        def _(eng):
            for fn in rec.ops["vector"]:
                fn(eng)

        @block.gpsimd
        def _(eng):
            for fn in rec.ops["gpsimd"]:
                fn(eng)

        @block.sync
        def _(eng):
            for fn in rec.ops["sync"]:
                fn(eng)

    for g in reversed(sem_guards):
        g.__exit__(None, None, None)
    return nc


_NC_CACHE = None


def _get_nc():
    global _NC_CACHE
    if _NC_CACHE is None:
        _NC_CACHE = build_nc()
    return _NC_CACHE


def kernel(L0_attn_x, L0_attn_Wq, L0_attn_Wk, L0_attn_Wv, L0_attn_Wo,
           L0_ffn_W0, L0_ffn_bias0, L0_ffn_W1, L0_ffn_bias1):
    x = np.asarray(L0_attn_x)
    w0t = np.ascontiguousarray(np.asarray(L0_ffn_W0).T)
    w1t = np.ascontiguousarray(np.asarray(L0_ffn_W1).T)
    wqt = np.ascontiguousarray(np.asarray(L0_attn_Wq).T)
    wkt = np.ascontiguousarray(np.asarray(L0_attn_Wk).T)
    wvt = np.ascontiguousarray(np.asarray(L0_attn_Wv).T)
    wot = np.ascontiguousarray(np.asarray(L0_attn_Wo).T)
    b0_l = np.ascontiguousarray(np.asarray(L0_ffn_bias0).reshape(64, 128).T)
    b1_l = np.ascontiguousarray(np.asarray(L0_ffn_bias1).reshape(16, 128).T)

    u = np.arange(128)[:, None]
    v = np.arange(128)[None, :]
    NEG = np.float32(-100000.0)
    m_lower = np.where(u > v, 0.0, NEG).astype(np.float32)
    m_diag = np.where(u <= v, 0.0, NEG).astype(np.float32)
    m_ones = np.zeros((128, 128), np.float32)
    m_zero = np.full((128, 128), NEG, np.float32)

    nc = _get_nc()
    in_maps = []
    for c in range(NCORES):
        xc = np.ascontiguousarray(x[0, c * T:(c + 1) * T, :].T)
        in_maps.append({
            "x": xc, "w0t": w0t, "w1t": w1t,
            "wqt": wqt, "wkt": wkt, "wvt": wvt, "wot": wot,
            "b0": b0_l, "b1": b1_l,
            "maskl": (m_zero if c == 0 else m_lower),
            "maskd": m_diag,
            "maskz": (m_zero if c == 0 else m_ones),
            "idx": np.array([[(c - 1) % NCORES]], np.int32),
        })
    res = run_bass_kernel_spmd(nc, in_maps, core_ids=list(range(NCORES)))
    out = np.empty((1, NCORES * T, H), np.float32)
    for c in range(NCORES):
        out[0, c * T:(c + 1) * T, :] = res.results[c]["y"].T
    return out


# version 21
# speedup vs baseline: 1.0707x; 1.0707x over previous
import numpy as np
import ml_dtypes
import concourse.bass as bass
import concourse.mybir as mybir
from concourse.bass_utils import run_bass_kernel_spmd

F32 = mybir.dt.float32
BF16 = mybir.dt.bfloat16
I32 = mybir.dt.int32
AF = mybir.ActivationFunctionType
ALU = mybir.AluOpType

T = 1024
H = 2048
HC = 16
NSUP = 16
NH = 16
SCALE = 1.0 / np.sqrt(128.0)
CAP = 30.0
EPS = 1e-5
NCORES = 8

A_COLS = 40960
A_W0 = [0, 8192]
A_W1 = [16384, 24576]
A_DT = [32768, 36864]
A_WQ = [0, 2048]
A_WK = [4096, 6144]
A_WV = [8192, 10240]
A_WO = [12288, 14336, 16384, 18432]
A_QOWN = [20480, 21504]
A_QST = [22528, 23552]
A_KST = [24576, 26624]
A_VST = [28672, 30720]
A_TS = [32768, 33792]
A_RB = 34816
A_SUM = 36864

B_COLS = 14336
B_KOWN = [0, 1024]
B_VFM = [2048, 3072]
B_VTM = [4096, 5120]
NATT = 5
B_ATT = [6144 + 1024 * i for i in range(NATT)]
B_P = [11264, 12288, 13312]
B_XB = [0, 1024, 2048, 3072]
B_XSQ = [4096, 5120, 6144, 7168]
B_MEAN = 8192
B_ROW2 = 10240
B_RSTD = 12288


def _i_span(jt):
    lo = max(0, jt - 8)
    hi = min(8, jt + 1)
    return lo, hi


class Ctr:
    def __init__(self):
        self.v = 0

    def bump(self, k=1):
        self.v += k
        return self.v


class Rec:
    def __init__(self):
        self.ops = {k: [] for k in ("tensor", "scalar", "vector", "gpsimd", "sync")}

    def on(self, eng, fn):
        self.ops[eng].append(fn)


def build_nc():
    nc = bass.Bass("TRN2", target_bir_lowering=False, debug=False,
                   dynamic_dma_scratch_size=16384)

    x_d = nc.dram_tensor("x", [H, T], F32, kind="ExternalInput").ap()
    w0t_d = nc.dram_tensor("w0t", [H, 8192], F32, kind="ExternalInput").ap()
    w1t_d = nc.dram_tensor("w1t", [8192, H], F32, kind="ExternalInput").ap()
    wqt_d = nc.dram_tensor("wqt", [H, H], F32, kind="ExternalInput").ap()
    wkt_d = nc.dram_tensor("wkt", [H, H], F32, kind="ExternalInput").ap()
    wvt_d = nc.dram_tensor("wvt", [H, H], F32, kind="ExternalInput").ap()
    wot_d = nc.dram_tensor("wot", [H, H], F32, kind="ExternalInput").ap()
    b0_d = nc.dram_tensor("b0", [128, 64], F32, kind="ExternalInput").ap()
    b1_d = nc.dram_tensor("b1", [128, 16], F32, kind="ExternalInput").ap()
    maskl_d = nc.dram_tensor("maskl", [128, 128], F32, kind="ExternalInput").ap()
    maskd_d = nc.dram_tensor("maskd", [128, 128], F32, kind="ExternalInput").ap()
    maskz_d = nc.dram_tensor("maskz", [128, 128], F32, kind="ExternalInput").ap()
    idx_d = nc.dram_tensor("idx", [1, 1], I32, kind="ExternalInput").ap()
    y_d = nc.dram_tensor("y", [H, T], F32, kind="ExternalOutput").ap()

    wq_bf = nc.dram_tensor("wq_bf", [H, H], BF16)
    wk_bf = nc.dram_tensor("wk_bf", [H, H], BF16)
    wv_bf = nc.dram_tensor("wv_bf", [H, H], BF16)
    wo_bf = nc.dram_tensor("wo_bf", [H, H], BF16)
    kvc = [nc.dram_tensor(f"kvc{i}", [4, 128, 2048], BF16) for i in range(4)]
    kvg = [nc.dram_tensor(f"kvg{i}", [NCORES, 4, 128, 2048], BF16,
                          addr_space="Shared") for i in range(4)]
    qh_d = nc.dram_tensor("qh", [NH, 128, T], BF16)

    ident_h = nc.inline_tensor(np.eye(128, dtype=ml_dtypes.bfloat16), name="ident")
    ones128_h = nc.inline_tensor(np.ones((128, 1), dtype=ml_dtypes.bfloat16),
                                 name="ones128")
    onesrow_h = nc.inline_tensor(np.ones((1, 128), dtype=np.float32),
                                 name="onesrow")
    onesmat_h = nc.inline_tensor(np.ones((128, 128), dtype=ml_dtypes.bfloat16),
                                 name="onesmat")

    x_sb = nc.alloc_sbuf_tensor("x_sb", [128, HC * T], F32).ap()
    xln = nc.alloc_sbuf_tensor("xln", [128, HC * T], BF16).ap()
    A = nc.alloc_sbuf_tensor("arenaA", [128, A_COLS], BF16).ap()
    B = nc.alloc_sbuf_tensor("arenaB", [128, B_COLS], BF16).ap()
    b0_sb = nc.alloc_sbuf_tensor("b0_sb", [128, 64], F32).ap()
    b1_sb = nc.alloc_sbuf_tensor("b1_sb", [128, 16], F32).ap()
    b1h_sb = nc.alloc_sbuf_tensor("b1h_sb", [128, 16], F32).ap()
    maskl_sb = nc.alloc_sbuf_tensor("maskl_sb", [128, 128], BF16).ap()
    maskd_sb = nc.alloc_sbuf_tensor("maskd_sb", [128, 128], BF16).ap()
    maskz_sb = nc.alloc_sbuf_tensor("maskz_sb", [128, 128], BF16).ap()
    ident_sb = nc.alloc_sbuf_tensor("ident_sb", [128, 128], BF16).ap()
    ones_sb = nc.alloc_sbuf_tensor("ones_sb", [128, 1], BF16).ap()
    onesrow_sb = nc.alloc_sbuf_tensor("onesrow_sb", [1, 128], F32).ap()
    onesmat_sb = nc.alloc_sbuf_tensor("onesmat_sb", [128, 128], BF16).ap()
    eps_sb = nc.alloc_sbuf_tensor("eps_sb", [1, 1], F32).ap()
    idx_sb = nc.alloc_sbuf_tensor("idx_sb", [1, 1], I32).ap()
    masks = {"l": maskl_sb, "d": maskd_sb, "z": maskz_sb}

    Bf = B.bitcast(F32)
    Af = A.bitcast(F32)

    def b_f32(col_bf16, n_f32, p0=0, p1=128):
        return Bf[p0:p1, col_bf16 // 2: col_bf16 // 2 + n_f32]

    pA = nc.alloc_psum_tensor("pA", [128, 512], F32).ap()
    pB = nc.alloc_psum_tensor("pB", [128, 512], F32).ap()
    pC = nc.alloc_psum_tensor("pC", [128, 512], F32).ap()
    pD = nc.alloc_psum_tensor("pD", [128, 512], F32).ap()
    psbig = nc.alloc_psum_tensor("psbig", [128, 1024], F32).ap()
    psbig2 = nc.alloc_psum_tensor("psbig2", [128, 1024], F32).ap()
    pCD = [pC, pD]
    pCbf = pC.bitcast(BF16)

    sem_names = ["s_x", "s_xc", "s_w", "s_mm1", "s_gel", "s_mm2", "s_add",
                 "s_lnxb", "s_lnsq", "s_lnmm", "s_lnrow", "s_lnbc", "s_lnap",
                 "s_aw", "s_pe", "s_kvout", "s_cc", "s_win", "s_st", "s_tanh",
                 "s_exp", "s_sum", "s_pv", "s_rcp", "s_rb", "s_av", "s_oproj",
                 "s_oadd", "s_out", "s_init", "s_qw", "s_pc", "s_scp", "s_avc", "s_lnm"]
    sems, ctrs, sem_guards = {}, {}, []
    for n in sem_names:
        g = nc.semaphore(n)
        sems[n] = g.__enter__()
        sem_guards.append(g)
        ctrs[n] = Ctr()

    rec = Rec()

    def S(n):
        return sems[n]

    def C(n):
        return ctrs[n]

    def _sync_init(e):
        xr = x_d.rearrange("(c p) t -> p c t", p=128)
        for hc in range(HC):
            e.dma_start(out=x_sb[:, hc * T:(hc + 1) * T],
                        in_=xr[:, hc, :]).then_inc(S("s_xc"), 16)
        e.dma_start(out=b0_sb[:, :], in_=b0_d[:, :]).then_inc(S("s_x"), 16)
        e.dma_start(out=b1_sb[:, :], in_=b1_d[:, :]).then_inc(S("s_x"), 16)
        e.dma_start(out=ident_sb[:, :], in_=ident_h.ap()[:, :]).then_inc(S("s_x"), 16)
        e.dma_start(out=ones_sb[:, :], in_=ones128_h.ap()[:, :]).then_inc(S("s_x"), 16)
        e.dma_start(out=onesrow_sb[:, :], in_=onesrow_h.ap()[:, :]).then_inc(S("s_x"), 16)
        e.dma_start(out=onesmat_sb[:, :], in_=onesmat_h.ap()[:, :]).then_inc(S("s_x"), 16)
        e.dma_start(out=idx_sb[:, :], in_=idx_d[:, :]).then_inc(S("s_x"), 16)
    rec.on("sync", _sync_init)
    C("s_xc").bump(16 * HC)
    C("s_x").bump(16 * 7)

    def _gp_init(e):
        e.dma_start(out=masks["l"][:, :], in_=maskl_d[:, :]).then_inc(S("s_x"), 16)
        e.dma_start(out=masks["d"][:, :], in_=maskd_d[:, :]).then_inc(S("s_x"), 16)
        e.dma_start(out=masks["z"][:, :], in_=maskz_d[:, :]).then_inc(S("s_x"), 16)
    rec.on("gpsimd", _gp_init)
    C("s_x").bump(16 * 3)
    x_loaded = C("s_x").v

    def _vec_init(e):
        e.wait_ge(S("s_x"), x_loaded)
        e.memset(eps_sb[:, :], EPS)
        e.tensor_scalar_mul(b1h_sb[:, :], b1_sb[:, :], 0.5).then_inc(S("s_init"), 1)
    rec.on("vector", _vec_init)
    C("s_init").bump(1)
    init_done = C("s_init").v

    def emit_ln(ready_sem, ready_val, chunk_gate=None, pre_info=None):
        xb = [B[:, B_XB[p]:B_XB[p] + T] for p in range(4)]
        xsq = [B[:, B_XSQ[p]:B_XSQ[p] + T] for p in range(4)]
        mean_r = b_f32(B_MEAN, T, 0, 1)
        row2 = b_f32(B_ROW2, T, 0, 1)
        rstd_r = b_f32(B_RSTD, T, 0, 1)

        if pre_info is not None:
            lnmm_base = pre_info["lnmm"]
            lnsq_base = pre_info["lnsq"]
        else:
            lnmm_base = C("s_lnmm").v
            lnsq_base = C("s_lnsq").v
        lnxb_base = C("s_lnxb").v

        def _vec_cast(e):
            for hc in range(HC):
                if chunk_gate is not None:
                    gs, gv = chunk_gate(hc)
                    e.wait_ge(S(gs), gv)
                elif hc == 0:
                    e.wait_ge(S(ready_sem), ready_val)
                if hc >= 2:
                    e.wait_ge(S("s_lnmm"), lnmm_base + hc - 1)
                e.tensor_copy(xb[hc % 2][:, :],
                              x_sb[:, hc * T:(hc + 1) * T])
                e.tensor_mul(xsq[hc % 2][:, :], xb[hc % 2][:, :],
                             xb[hc % 2][:, :]).then_inc(S("s_lnsq"), 1)
        if pre_info is None:
            rec.on("vector", _vec_cast)
            C("s_lnsq").bump(HC)

        def _ten_stats(e):
            for hc in range(HC):
                if hc == 0:
                    e.wait_ge(S("s_x"), x_loaded)
                    if chunk_gate is not None:
                        e.wait_ge(S(ready_sem), ready_val)
                e.wait_ge(S("s_lnsq"), lnsq_base + hc + 1)
                for ih in range(2):
                    e.matmul(psbig[0:1, ih * 512:(ih + 1) * 512], ones_sb[:, :],
                             xb[hc % 2][:, ih * 512:(ih + 1) * 512],
                             start=(hc == 0), stop=(hc == HC - 1))
                ins = None
                for ih in range(2):
                    ins = e.matmul(psbig2[0:1, ih * 512:(ih + 1) * 512],
                                   ones_sb[:, :],
                                   xsq[hc % 2][:, ih * 512:(ih + 1) * 512],
                                   start=(hc == 0), stop=(hc == HC - 1))
                ins.then_inc(S("s_lnmm"), 1)
        if pre_info is None:
            rec.on("tensor", _ten_stats)
            C("s_lnmm").bump(HC)
        stats_done = lnmm_base + HC

        def _vec_rows(e):
            e.wait_ge(S("s_lnmm"), stats_done)
            e.tensor_scalar_mul(mean_r, psbig[0:1, :], 1.0 / H).then_inc(S("s_lnm"), 1)
            e.tensor_scalar_mul(row2, psbig2[0:1, :], 1.0 / H)
            e.tensor_mul(rstd_r, mean_r, mean_r)
            e.tensor_sub(row2, row2, rstd_r).then_inc(S("s_lnrow"), 1)
        rec.on("vector", _vec_rows)
        C("s_lnm").bump(1)
        mean_done = C("s_lnm").v
        C("s_lnrow").bump(1)
        var_done = C("s_lnrow").v

        def _ten_bcM(e):
            e.wait_ge(S("s_lnm"), mean_done)
            ins = None
            for ih in range(2):
                ins = e.matmul(psbig[:, ih * 512:(ih + 1) * 512], onesrow_sb[:, :],
                               mean_r[:, ih * 512:(ih + 1) * 512],
                               start=True, stop=True)
            ins.then_inc(S("s_lnbc"), 1)
        rec.on("tensor", _ten_bcM)
        C("s_lnbc").bump(1)
        bcm_done = C("s_lnbc").v

        def _vec_pass1(e):
            e.wait_ge(S("s_lnbc"), bcm_done)
            for hc in range(HC):
                e.tensor_sub(xln[:, hc * T:(hc + 1) * T],
                             x_sb[:, hc * T:(hc + 1) * T], psbig[:, :])
            e.engine_nop().then_inc(S("s_lnrow"), 1)
        rec.on("vector", _vec_pass1)
        C("s_lnrow").bump(1)

        def _sca_sqrt(e):
            e.wait_ge(S("s_lnrow"), var_done)
            e.activation(out=row2, in_=row2, func=AF.Sqrt,
                         bias=eps_sb[0:1, :]).then_inc(S("s_lnrow"), 1)
        rec.on("scalar", _sca_sqrt)
        C("s_lnrow").bump(1)
        sqrt_done = C("s_lnrow").v

        def _vec_recip(e):
            e.wait_ge(S("s_lnrow"), sqrt_done)
            e.reciprocal(rstd_r, row2).then_inc(S("s_lnrow"), 1)
        rec.on("vector", _vec_recip)
        C("s_lnrow").bump(1)
        rows_done = C("s_lnrow").v

        def _ten_bcR(e):
            e.wait_ge(S("s_lnrow"), rows_done)
            ins = None
            for ih in range(2):
                ins = e.matmul(psbig2[:, ih * 512:(ih + 1) * 512], onesrow_sb[:, :],
                               rstd_r[:, ih * 512:(ih + 1) * 512],
                               start=True, stop=True)
            ins.then_inc(S("s_lnbc"), 1)
        rec.on("tensor", _ten_bcR)
        C("s_lnbc").bump(1)
        bc_done = C("s_lnbc").v

        def _vec_apply(e):
            e.wait_ge(S("s_lnbc"), bc_done)
            for hc in range(HC):
                e.tensor_mul(xln[:, hc * T:(hc + 1) * T],
                             xln[:, hc * T:(hc + 1) * T],
                             psbig2[:, :]).then_inc(S("s_lnap"), 1)
        rec.on("vector", _vec_apply)
        C("s_lnap").bump(HC)
        return C("s_lnap").v

    def emit_ffn(ln_done, region_ready, gp_hook=None, early_gates=None,
                 ln_hooks=None):
        w0 = [A[:, A_W0[p]:A_W0[p] + 8192] for p in range(2)]
        w1 = [A[:, A_W1[p]:A_W1[p] + 8192] for p in range(2)]
        dt = [A[:, A_DT[p]:A_DT[p] + 4096] for p in range(2)]

        w_base = C("s_w").v
        mm1_base = C("s_mm1").v
        gel_base = C("s_gel").v
        mm2_base = C("s_mm2").v
        add_base = C("s_add").v

        def _vec_b1(e):
            e.wait_ge(S("s_init"), init_done)
            for hc in range(HC):
                e.tensor_scalar_add(x_sb[:, hc * T:(hc + 1) * T],
                                    x_sb[:, hc * T:(hc + 1) * T],
                                    b1h_sb[:, hc:hc + 1])
            e.engine_nop().then_inc(S("s_add"), 1)
        rec.on("vector", _vec_b1)
        C("s_add").bump(1)

        def _gp_w(e):
            if region_ready is not None:
                e.wait_ge(S(region_ready[0]), region_ready[1])
            for s in range(NSUP):
                if gp_hook is not None:
                    gp_hook(e, s)
                if s >= 2:
                    e.wait_ge(S("s_mm2"), mm2_base + 32 * (s - 1))
                if early_gates is not None and s in early_gates:
                    for gsem, gval in early_gates[s]:
                        e.wait_ge(S(gsem), gval)
                p = s % 2
                e.dma_start(
                    out=w0[p].rearrange("p (c n) -> p c n", n=512),
                    in_=w0t_d.rearrange("(c p) n -> p c n", p=128)[:, :, s * 512:(s + 1) * 512],
                ).then_inc(S("s_w"), 16)
                e.dma_start(
                    out=w1[p].rearrange("p (c h) -> p c h", h=H),
                    in_=w1t_d.rearrange("(S c p) h -> S p c h", p=128, c=4)[s],
                ).then_inc(S("s_w"), 16)
        rec.on("gpsimd", _gp_w)
        C("s_w").bump(32 * NSUP)

        def _ten(e):
            for s in range(NSUP):
                p = s % 2
                for nc4 in range(4):
                    for ih in range(2):
                        tile = s * 8 + nc4 * 2 + ih
                        ps = pA if tile % 2 == 0 else pB
                        if nc4 == 0 and ih == 0:
                            e.wait_ge(S("s_w"), w_base + 32 * (s + 1))
                        if tile >= 2:
                            e.wait_ge(S("s_gel"), gel_base + tile - 1)
                        ins = None
                        for hc in range(HC):
                            if s == 0 and nc4 == 0 and ih == 0:
                                e.wait_ge(S("s_lnap"), ln_done - HC + hc + 1)
                            ins = e.matmul(
                                ps[:, :],
                                w0[p][:, hc * 512 + nc4 * 128:hc * 512 + (nc4 + 1) * 128],
                                xln[:, hc * T + ih * 512:hc * T + (ih + 1) * 512],
                                start=(hc == 0), stop=(hc == HC - 1))
                        ins.then_inc(S("s_mm1"), 1)
                last = (s == NSUP - 1) and (ln_hooks is not None)
                for hcc in range(HC):
                    for ih2 in range(2):
                        et = s * 32 + hcc * 2 + ih2
                        if last:
                            psE = pCD[ih2]
                            prev_et = et - 4 if et < s * 32 + 2 else et - 2
                            e.wait_ge(S("s_add"), add_base + prev_et + 2)
                        else:
                            rot = et % 4
                            psE = (pC, pD, psbig[:, 0:512], psbig[:, 512:1024])[rot]
                            if et >= 4:
                                e.wait_ge(S("s_add"), add_base + et - 2)
                        if hcc == 0 and ih2 == 0:
                            e.wait_ge(S("s_gel"), gel_base + 8 * (s + 1))
                        ins = None
                        for nc4 in range(4):
                            ins = e.matmul(
                                psE,
                                w1[p][:, nc4 * 2048 + hcc * 128:nc4 * 2048 + (hcc + 1) * 128],
                                dt[p][:, nc4 * 1024 + ih2 * 512:nc4 * 1024 + (ih2 + 1) * 512],
                                start=(nc4 == 0), stop=(nc4 == 3))
                        ins.then_inc(S("s_mm2"), 1)
                        if last and ih2 == 1:
                            ln_hooks["ten"](e, hcc)
        rec.on("tensor", _ten)
        C("s_mm1").bump(8 * NSUP)
        C("s_mm2").bump(32 * NSUP)

        def _sca_gelu(e):
            for s in range(NSUP):
                p = s % 2
                for nc4 in range(4):
                    for ih in range(2):
                        tile = s * 8 + nc4 * 2 + ih
                        ps = pA if tile % 2 == 0 else pB
                        if s >= 2 and nc4 == 0 and ih == 0:
                            e.wait_ge(S("s_mm2"), mm2_base + 32 * (s - 1))
                        e.wait_ge(S("s_mm1"), mm1_base + tile + 1)
                        e.activation(
                            out=dt[p][:, nc4 * 1024 + ih * 512:nc4 * 1024 + (ih + 1) * 512],
                            in_=ps[:, :], func=AF.Gelu,
                            bias=b0_sb[:, s * 4 + nc4:s * 4 + nc4 + 1],
                        ).then_inc(S("s_gel"), 1)
        rec.on("scalar", _sca_gelu)
        C("s_gel").bump(8 * NSUP)

        def _vec_add(e):
            for s in range(NSUP):
                for hcc in range(HC):
                    for ih2 in range(2):
                        et = s * 32 + hcc * 2 + ih2
                        if (s == NSUP - 1) and (ln_hooks is not None):
                            psE = pCD[ih2]
                        else:
                            rot = et % 4
                            psE = (pC, pD, psbig[:, 0:512], psbig[:, 512:1024])[rot]
                        e.wait_ge(S("s_mm2"), mm2_base + et + 1)
                        xs = x_sb[:, hcc * T + ih2 * 512:hcc * T + (ih2 + 1) * 512]
                        e.scalar_tensor_tensor(
                            out=xs, in0=psE, scalar=0.5, in1=xs,
                            op0=ALU.mult, op1=ALU.add,
                        ).then_inc(S("s_add"), 1)
                        if (s == NSUP - 1) and (ln_hooks is not None) and ih2 == 1:
                            ln_hooks["vec"](e, hcc)
        rec.on("vector", _vec_add)
        C("s_add").bump(32 * NSUP)
        return C("s_add").v

    def make_ln_pre():
        xb = [B[:, B_XB[p]:B_XB[p] + T] for p in range(2)]
        xsq = [B[:, B_XSQ[p]:B_XSQ[p] + T] for p in range(2)]
        info = {"lnsq": C("s_lnsq").v, "lnmm": C("s_lnmm").v}

        def vec_hook(e, hc):
            e.tensor_copy(xb[hc % 2][:, :], x_sb[:, hc * T:(hc + 1) * T])
            e.tensor_mul(xsq[hc % 2][:, :], xb[hc % 2][:, :],
                         xb[hc % 2][:, :]).then_inc(S("s_lnsq"), 1)

        def ten_hook(e, hc):
            e.wait_ge(S("s_lnsq"), info["lnsq"] + hc + 1)
            for ih in range(2):
                e.matmul(psbig[0:1, ih * 512:(ih + 1) * 512], ones_sb[:, :],
                         xb[hc % 2][:, ih * 512:(ih + 1) * 512],
                         start=(hc == 0), stop=(hc == HC - 1))
            ins = None
            for ih in range(2):
                ins = e.matmul(psbig2[0:1, ih * 512:(ih + 1) * 512],
                               ones_sb[:, :],
                               xsq[hc % 2][:, ih * 512:(ih + 1) * 512],
                               start=(hc == 0), stop=(hc == HC - 1))
            ins.then_inc(S("s_lnmm"), 1)

        C("s_lnsq").bump(HC)
        C("s_lnmm").bump(HC)
        return {"ten": ten_hook, "vec": vec_hook}, info

    def emit_attn(ln_done, ffn1_add_done, ln_hooks=None):
        wq = [A[:, A_WQ[p]:A_WQ[p] + 2048] for p in range(2)]
        wk = [A[:, A_WK[p]:A_WK[p] + 2048] for p in range(2)]
        wv = [A[:, A_WV[p]:A_WV[p] + 2048] for p in range(2)]
        wo = [A[:, A_WO[p]:A_WO[p] + 2048] for p in range(4)]
        qown = [A[:, A_QOWN[p]:A_QOWN[p] + 1024] for p in range(2)]
        qst = [A[:, A_QST[p]:A_QST[p] + 1024] for p in range(2)]
        kst = [A[:, A_KST[p]:A_KST[p] + 2048] for p in range(2)]
        vst = [A[:, A_VST[p]:A_VST[p] + 2048] for p in range(2)]
        ts = [Af[:, A_TS[p] // 2: A_TS[p] // 2 + 512] for p in range(2)]
        kown = [B[:, B_KOWN[p]:B_KOWN[p] + 1024] for p in range(2)]
        vfm = [B[:, B_VFM[p]:B_VFM[p] + 1024] for p in range(2)]
        vtm = [B[:, B_VTM[p]:B_VTM[p] + 1024] for p in range(2)]
        att = [B[:, B_ATT[i]:B_ATT[i] + 1024] for i in range(NATT)]
        psl = [B[:, B_P[p]:B_P[p] + 1024] for p in range(3)]
        rb = Af[:, A_RB // 2: A_RB // 2 + 1024]
        sum_r = Af[0:1, A_SUM // 2: A_SUM // 2 + 1024]

        aw_base = C("s_aw").v
        qw_base = C("s_qw").v
        pe_base = C("s_pe").v
        st_base = C("s_st").v
        kvout_base = C("s_kvout").v
        cc_base = C("s_cc").v
        win_base = C("s_win").v
        tanh_base = C("s_tanh").v
        exp_base = C("s_exp").v
        sum_base = C("s_sum").v
        pv_base = C("s_pv").v
        rcp_base = C("s_rcp").v
        scp_base = C("s_scp").v
        avc_base = C("s_avc").v
        rb_base = C("s_rb").v
        av_base = C("s_av").v
        oproj_base = C("s_oproj").v
        oadd_base = C("s_oadd").v

        def pe_val(h, k):
            return pe_base + 7 * h + k + 1

        def st_val(h, k):
            return st_base + 7 * h + k + 1

        wqt_r = wq_bf.ap().rearrange("(c p) (G o) -> G p c o", p=128, o=128)
        wkt_r = wk_bf.ap().rearrange("(c p) (G o) -> G p c o", p=128, o=128)
        wvt_r = wv_bf.ap().rearrange("(c p) (G o) -> G p c o", p=128, o=128)
        wot_r = wo_bf.ap().rearrange("(G p) g -> G p g", p=128)

        def head_subs():
            half0 = [(jt, 0, max(0, jt - 8), min(4, jt + 1)) for jt in range(0, 12)]
            half1 = [(jt, 1, max(4, jt - 8), min(8, jt + 1)) for jt in range(4, 16)]
            out = []
            for a, b in zip(half0, half1):
                out.append(a)
                out.append(b)
            return out

        subs = []
        for h in range(NH):
            for (jt, half, l2, h2) in head_subs():
                subs.append((h, jt, half, l2, h2, jt == (0 if half == 0 else 4)))
        NSUBH = len(subs) // NH
        prev_parity = {}
        last_par = {}
        for si, (h, jt, half, l2, h2, stf) in enumerate(subs):
            key = (jt % 3, half)
            prev_parity[si] = last_par.get(key)
            last_par[key] = si

        def mask_ops(jt, l2, h2):
            out = []
            for ic in range(l2, h2):
                rt = jt - ic
                if rt == 0:
                    out.append((ic, "l"))
                elif rt == 8:
                    out.append((ic, "d"))
                elif jt < 8:
                    out.append((ic, "z"))
            return out

        def _sync_attn1(e):
            def slices(h):
                p = h % 2
                if h >= 2:
                    e.wait_ge(S("s_st"), st_val(h - 2, 6))
                e.dma_start(out=wq[p].rearrange("p (c o) -> p c o", o=128),
                            in_=wqt_r[h]).then_inc(S("s_aw"), 16)
                e.dma_start(out=wk[p].rearrange("p (c o) -> p c o", o=128),
                            in_=wkt_r[h]).then_inc(S("s_aw"), 16)
                e.dma_start(out=wv[p].rearrange("p (c o) -> p c o", o=128),
                            in_=wvt_r[h]).then_inc(S("s_aw"), 16)

            e.wait_ge(S("s_add"), ffn1_add_done)
            e.wait_ge(S("s_pc"), 256)
            e.wait_ge(S("s_x"), x_loaded)
            prev = e.value_load(idx_sb[0:1, 0:1])

            def kv(h):
                p = h % 2
                g, hh = divmod(h, 4)
                e.wait_ge(S("s_pe"), pe_val(h, 3))
                e.dma_start(out=qh_d.ap()[h], in_=qown[p][:, :]).then_inc(S("s_kvout"), 16)
                e.dma_start(out=kvc[g].ap()[hh, :, 0:1024],
                            in_=kown[p][:, :]).then_inc(S("s_kvout"), 16)
                e.wait_ge(S("s_pe"), pe_val(h, 6))
                e.dma_start(out=kvc[g].ap()[hh, :, 1024:2048],
                            in_=vtm[p][:, :]).then_inc(S("s_kvout"), 16)

            def wowin(h):
                p = h % 2
                g, hh = divmod(h, 4)
                if h >= 4:
                    e.wait_ge(S("s_oproj"),
                              oproj_base + 32 * ((h - 4) // 4 + 1))
                e.dma_start(out=wo[h % 4][:, :],
                            in_=wot_r[h]).then_inc(S("s_qw"), 16)
                e.wait_ge(S("s_cc"), cc_base + g + 1)
                if h >= 2:
                    e.wait_ge(S("s_pv"), pv_base + NSUBH * (h - 1))
                kv_halo = kvg[g].ap()[bass.ts(prev, 1)].squeeze(0)
                e.dma_start(out=kst[p][:, 0:1024],
                            in_=kv_halo[hh, :, 0:1024]).then_inc(S("s_win"), 16)
                e.dma_start(out=vst[p][:, 0:1024],
                            in_=kv_halo[hh, :, 1024:2048]).then_inc(S("s_win"), 16)
                e.dma_start(out=kst[p][:, 1024:2048],
                            in_=kvc[g].ap()[hh, :, 0:1024]).then_inc(S("s_win"), 16)
                e.dma_start(out=vst[p][:, 1024:2048],
                            in_=kvc[g].ap()[hh, :, 1024:2048]).then_inc(S("s_win"), 16)
                e.dma_start(out=qst[p][:, :],
                            in_=qh_d.ap()[h]).then_inc(S("s_win"), 16)

            slices(0)
            slices(1)
            for h in range(NH):
                if h + 2 < NH:
                    slices(h + 2)
                kv(h)
            for h in range(NH):
                wowin(h)
        rec.on("sync", _sync_attn1)
        C("s_aw").bump(48 * NH)
        C("s_kvout").bump(48 * NH)

        def _gp_cc(e):
            for g in range(4):
                e.wait_ge(S("s_kvout"), kvout_base + 48 * 4 * (g + 1))
                e.collective_compute(
                    "AllGather", ALU.bypass,
                    replica_groups=[list(range(NCORES))],
                    ins=[kvc[g].ap().opt()],
                    outs=[kvg[g].ap().opt()],
                ).then_inc(S("s_cc"), 1)
        rec.on("gpsimd", _gp_cc)
        C("s_cc").bump(4)

        C("s_qw").bump(16 * NH)
        C("s_win").bump(80 * NH)

        def _ten_proj(e):
            for h in range(NH):
                p = h % 2
                e.wait_ge(S("s_aw"), aw_base + 48 * (h + 1))
                for wi, wsl in ((0, wq[p]), (1, wk[p]), (2, wv[p])):
                    for th in range(2):
                        t6 = h * 6 + wi * 2 + th
                        ps = pCD[t6 % 2]
                        if t6 >= 2:
                            pt = t6 - 2
                            e.wait_ge(S("s_pe"), pe_val(pt // 6, pt % 6))
                        if wi == 0 and th == 0 and h >= 1:
                            e.wait_ge(S("s_pe"), pe_val(h - 1, 6))
                        ins = None
                        for hc in range(HC):
                            if t6 == 0:
                                e.wait_ge(S("s_lnap"), ln_done - HC + hc + 1)
                            ins = e.matmul(
                                ps[:, :],
                                wsl[:, hc * 128:(hc + 1) * 128],
                                xln[:, hc * T + th * 512:hc * T + (th + 1) * 512],
                                start=(hc == 0), stop=(hc == HC - 1))
                        ins.then_inc(S("s_st"), 1)
                e.wait_ge(S("s_pe"), pe_val(h, 5))
                ins = None
                for tc in range(8):
                    ins = e.transpose(pCbf[:, tc * 128:(tc + 1) * 128],
                                      vfm[p][:, tc * 128:(tc + 1) * 128],
                                      ident_sb[:, :])
                ins.then_inc(S("s_st"), 1)
        rec.on("tensor", _ten_proj)
        C("s_st").bump(7 * NH)

        def _sca_pe(e):
            for h in range(NH):
                p = h % 2
                for wi, dst in ((0, qown[p]), (1, kown[p]), (2, vfm[p])):
                    for th in range(2):
                        t6 = h * 6 + wi * 2 + th
                        e.wait_ge(S("s_st"), st_val(h, wi * 2 + th))
                        if wi == 0 and th == 0 and h >= 2:
                            e.wait_ge(S("s_kvout"), kvout_base + 48 * (h - 1))
                        e.copy(out=dst[:, th * 512:(th + 1) * 512],
                               in_=pCD[t6 % 2][:, :]).then_inc(S("s_pe"), 1)
        rec.on("scalar", _sca_pe)
        C("s_pe").bump(6 * NH)

        def _vec_vtm(e):
            for h in range(NH):
                p = h % 2
                e.wait_ge(S("s_st"), st_val(h, 6))
                if h >= 2:
                    e.wait_ge(S("s_kvout"), kvout_base + 48 * (h - 1))
                e.tensor_copy(vtm[p][:, :], pCbf[:, :]).then_inc(S("s_pe"), 1)
        rec.on("vector", _vec_vtm)
        C("s_pe").bump(NH)

        def _ten_scores(e):
            def do_st(si):
                h, jt, half, l2, h2, stf = subs[si]
                p = h % 2
                ps = pA if si % 2 == 0 else pB
                span = (h2 - l2) * 128
                k = si % NSUBH
                if k == 0:
                    e.wait_ge(S("s_win"), win_base + 80 * (h + 1))
                if si >= 2:
                    e.wait_ge(S("s_tanh"), tanh_base + si - 1)
                mo = mask_ops(jt, l2, h2)
                ins = e.matmul(ps[:, 0:span], kst[p][:, jt * 128:(jt + 1) * 128],
                               qst[p][:, l2 * 128:h2 * 128],
                               start=True, stop=(not mo))
                for mi, (ic, mk) in enumerate(mo):
                    r0 = (ic - l2) * 128
                    ins = e.matmul(ps[:, r0:r0 + 128], ident_sb[:, :],
                                   masks[mk][:, :], start=False,
                                   stop=(mi == len(mo) - 1))
                ins.then_inc(S("s_st"), 1)

            def do_sumpv(si):
                h, jt, half, l2, h2, stf = subs[si]
                p = h % 2
                k = si % NSUBH
                e.wait_ge(S("s_exp"), exp_base + si + 1)
                if k == 0 and h > 0:
                    e.wait_ge(S("s_rcp"), rcp_base + h)
                e.matmul(psbig2[:, l2 * 128:h2 * 128], onesmat_sb[:, :],
                         psl[jt % 3][:, l2 * 128:h2 * 128],
                         start=stf, stop=False).then_inc(S("s_sum"), 1)
                if k == 0 and h > 0:
                    e.wait_ge(S("s_avc"), avc_base + h)
                e.matmul(psbig[:, l2 * 128:h2 * 128],
                         vst[p][:, jt * 128:(jt + 1) * 128],
                         psl[jt % 3][:, l2 * 128:h2 * 128],
                         start=stf, stop=False).then_inc(S("s_pv"), 1)

            def do_oproj(g):
                e.wait_ge(S("s_av"), av_base + 4 * (g + 1))
                e.wait_ge(S("s_qw"), qw_base + 64 * (g + 1))
                if g == 0:
                    e.wait_ge(S("s_pe"), pe_val(NH - 1, 6))
                for g2 in range(HC):
                    for th in range(2):
                        ot = g * 32 + g2 * 2 + th
                        pso = pCD[ot % 2]
                        if ot >= 2:
                            e.wait_ge(S("s_oadd"), oadd_base + ot - 1)
                        elif g > 0:
                            e.wait_ge(S("s_oadd"), oadd_base + 32 * g)
                        ins = None
                        for i in range(4):
                            hh = 4 * g + i
                            ins = e.matmul(
                                pso[:, :],
                                wo[hh % 4][:, g2 * 128:(g2 + 1) * 128],
                                att[hh % NATT][:, th * 512:(th + 1) * 512],
                                start=(i == 0), stop=(i == 3))
                        ins.then_inc(S("s_oproj"), 1)
                        if g == 3 and th == 1 and ln_hooks is not None:
                            ln_hooks["ten"](e, g2)

            DELAY = 3
            for h in range(NH):
                for k in range(NSUBH):
                    si = h * NSUBH + k
                    do_st(si)
                    if k >= DELAY:
                        do_sumpv(si - DELAY)
                for k in range(NSUBH - DELAY, NSUBH):
                    do_sumpv(h * NSUBH + k)
                if h % 4 == 3:
                    do_oproj(h // 4)
        rec.on("tensor", _ten_scores)
        C("s_st").bump(len(subs))
        C("s_sum").bump(len(subs))
        C("s_pv").bump(len(subs))
        C("s_oproj").bump(128)

        def _sca_scores(e):
            for si, (h, jt, half, l2, h2, stf) in enumerate(subs):
                pq = si % 2
                ps = pA if pq == 0 else pB
                span = (h2 - l2) * 128
                e.wait_ge(S("s_st"), st_base + 7 * NH + si + 1)
                e.activation(out=ts[pq][:, 0:span], in_=ps[:, 0:span],
                             func=AF.Tanh, scale=float(SCALE / CAP)
                             ).then_inc(S("s_tanh"), 1)
                pp = prev_parity[si]
                if pp is not None:
                    e.wait_ge(S("s_pv"), pv_base + pp + 1)
                e.activation(out=psl[jt % 3][:, l2 * 128:h2 * 128],
                             in_=ts[pq][:, 0:span],
                             func=AF.Exp, scale=float(CAP)).then_inc(S("s_exp"), 1)
                if si % NSUBH == NSUBH - 1:
                    e.wait_ge(S("s_pv"), pv_base + NSUBH * (h + 1))
                    if h >= NATT:
                        e.wait_ge(S("s_oproj"),
                                  oproj_base + 32 * ((h - NATT) // 4 + 1))
                    e.copy(out=att[h % NATT][:, :],
                           in_=psbig[:, :]).then_inc(S("s_avc"), 1)
        rec.on("scalar", _sca_scores)
        C("s_tanh").bump(len(subs))
        C("s_exp").bump(len(subs))
        C("s_avc").bump(NH)

        def _vec_scores(e):
            for si, (h, jt, half, l2, h2, stf) in enumerate(subs):
                if si % NSUBH == NSUBH - 1:
                    e.wait_ge(S("s_sum"), sum_base + NSUBH * (h + 1))
                    e.tensor_copy(rb[:, 0:512], psbig2[:, 0:512])
                    e.tensor_copy(rb[:, 512:1024],
                                  psbig2[:, 512:1024]).then_inc(S("s_rcp"), 1)
                    e.reciprocal(rb, rb)
                    e.wait_ge(S("s_avc"), avc_base + h + 1)
                    e.scalar_tensor_tensor(
                        out=att[h % NATT][:, :], in0=att[h % NATT][:, :],
                        scalar=1.0, in1=rb, op0=ALU.mult, op1=ALU.mult,
                    ).then_inc(S("s_av"), 1)
                    if h % 4 == 3:
                        g = h // 4
                        for g2 in range(HC):
                            for th in range(2):
                                ot = g * 32 + g2 * 2 + th
                                pso = pCD[ot % 2]
                                e.wait_ge(S("s_oproj"), oproj_base + ot + 1)
                                xs = x_sb[:, g2 * T + th * 512:g2 * T + (th + 1) * 512]
                                e.tensor_add(xs, xs, pso[:, :]).then_inc(S("s_oadd"), 1)
                                if g == 3 and th == 1 and ln_hooks is not None:
                                    ln_hooks["vec"](e, g2)
        rec.on("vector", _vec_scores)
        C("s_rcp").bump(NH)
        C("s_av").bump(NH)
        C("s_oadd").bump(128)
        return {
            "oadd": C("s_oadd").v,
            "proj_st": st_base + 7 * NH,
            "oproj": oproj_base + 128,
            "kvout": kvout_base + 48 * NH,
            "win": win_base + 80 * NH,
            "pv": pv_base + NSUBH * NH,
        }

    ln1_done = emit_ln("s_xc", 16 * HC,
                       chunk_gate=lambda hc: ("s_xc", 16 * (hc + 1)))
    _pc_pairs = [(dst, src) for dst, src in
                 ((wq_bf, wqt_d), (wk_bf, wkt_d), (wv_bf, wvt_d), (wo_bf, wot_d))]

    def _precast(e, sidx):
        for k in range(2):
            i = (sidx - 2) * 2 + k
            if 0 <= i < 16:
                dst, src = _pc_pairs[i // 4]
                r0 = (i % 4) * 512
                e.dma_start(out=dst.ap()[r0:r0 + 512, :],
                            in_=src[r0:r0 + 512, :]).then_inc(S("s_pc"), 16)
    C("s_pc").bump(256)

    ln2h, ln2i = make_ln_pre()
    ffn1_done = emit_ffn(ln1_done, None, gp_hook=_precast, ln_hooks=ln2h)
    ln2_done = emit_ln("s_add", ffn1_done, pre_info=ln2i,
                       chunk_gate=lambda hc: ("s_add", ffn1_done - 30 + 2 * hc))
    ainfo = emit_attn(ln2_done, ffn1_done)
    attn_done = ainfo["oadd"]
    ln3_done = emit_ln("s_oadd", attn_done,
                       chunk_gate=lambda hc: ("s_oadd", attn_done - 30 + 2 * hc))
    ffn2_done = emit_ffn(ln3_done, None, early_gates={
        0: [("s_st", ainfo["proj_st"]), ("s_oproj", ainfo["oproj"]),
            ("s_kvout", ainfo["kvout"]), ("s_pv", ainfo["pv"])],
        1: [("s_pv", ainfo["pv"]), ("s_win", ainfo["win"]),
            ("s_oproj", ainfo["oproj"])],
    })

    def _sync_out(e):
        e.wait_ge(S("s_add"), ffn2_done)
        e.dma_start(out=y_d.rearrange("(c p) t -> p c t", p=128),
                    in_=x_sb.rearrange("p (c t) -> p c t", t=T)
                    ).then_inc(S("s_out"), 16)
        e.wait_ge(S("s_out"), 16)
    rec.on("sync", _sync_out)
    C("s_out").bump(16)

    with nc.Block() as block:
        @block.tensor
        def _(eng):
            for fn in rec.ops["tensor"]:
                fn(eng)

        @block.scalar
        def _(eng):
            for fn in rec.ops["scalar"]:
                fn(eng)

        @block.vector
        def _(eng):
            for fn in rec.ops["vector"]:
                fn(eng)

        @block.gpsimd
        def _(eng):
            for fn in rec.ops["gpsimd"]:
                fn(eng)

        @block.sync
        def _(eng):
            for fn in rec.ops["sync"]:
                fn(eng)

    for g in reversed(sem_guards):
        g.__exit__(None, None, None)
    return nc


_NC_CACHE = None


def _get_nc():
    global _NC_CACHE
    if _NC_CACHE is None:
        _NC_CACHE = build_nc()
    return _NC_CACHE


def kernel(L0_attn_x, L0_attn_Wq, L0_attn_Wk, L0_attn_Wv, L0_attn_Wo,
           L0_ffn_W0, L0_ffn_bias0, L0_ffn_W1, L0_ffn_bias1):
    x = np.asarray(L0_attn_x)
    w0t = np.ascontiguousarray(np.asarray(L0_ffn_W0).T)
    w1t = np.ascontiguousarray(np.asarray(L0_ffn_W1).T)
    wqt = np.ascontiguousarray(np.asarray(L0_attn_Wq).T)
    wkt = np.ascontiguousarray(np.asarray(L0_attn_Wk).T)
    wvt = np.ascontiguousarray(np.asarray(L0_attn_Wv).T)
    wot = np.ascontiguousarray(np.asarray(L0_attn_Wo).T)
    b0_l = np.ascontiguousarray(np.asarray(L0_ffn_bias0).reshape(64, 128).T)
    b1_l = np.ascontiguousarray(np.asarray(L0_ffn_bias1).reshape(16, 128).T)

    u = np.arange(128)[:, None]
    v = np.arange(128)[None, :]
    NEG = np.float32(-100000.0)
    m_lower = np.where(u > v, 0.0, NEG).astype(np.float32)
    m_diag = np.where(u <= v, 0.0, NEG).astype(np.float32)
    m_ones = np.zeros((128, 128), np.float32)
    m_zero = np.full((128, 128), NEG, np.float32)

    nc = _get_nc()
    in_maps = []
    for c in range(NCORES):
        xc = np.ascontiguousarray(x[0, c * T:(c + 1) * T, :].T)
        in_maps.append({
            "x": xc, "w0t": w0t, "w1t": w1t,
            "wqt": wqt, "wkt": wkt, "wvt": wvt, "wot": wot,
            "b0": b0_l, "b1": b1_l,
            "maskl": (m_zero if c == 0 else m_lower),
            "maskd": m_diag,
            "maskz": (m_zero if c == 0 else m_ones),
            "idx": np.array([[(c - 1) % NCORES]], np.int32),
        })
    res = run_bass_kernel_spmd(nc, in_maps, core_ids=list(range(NCORES)))
    out = np.empty((1, NCORES * T, H), np.float32)
    for c in range(NCORES):
        out[0, c * T:(c + 1) * T, :] = res.results[c]["y"].T
    return out


# version 22
# speedup vs baseline: 1.0923x; 1.0201x over previous
import numpy as np
import ml_dtypes
import concourse.bass as bass
import concourse.mybir as mybir
from concourse.bass_utils import run_bass_kernel_spmd

F32 = mybir.dt.float32
BF16 = mybir.dt.bfloat16
I32 = mybir.dt.int32
AF = mybir.ActivationFunctionType
ALU = mybir.AluOpType

T = 1024
H = 2048
HC = 16
NSUP = 16
NH = 16
SCALE = 1.0 / np.sqrt(128.0)
CAP = 30.0
EPS = 1e-5
NCORES = 8

A_COLS = 40960
A_W0 = [0, 8192]
A_W1 = [16384, 24576]
A_DT = [32768, 36864]
A_WQ = [0, 2048]
A_WK = [4096, 6144]
A_WV = [8192, 10240]
A_WO = [12288, 14336, 16384, 18432]
A_QOWN = [20480, 21504]
A_QST = [22528, 23552]
A_KST = [24576, 26624]
A_VST = [28672, 30720]
A_TS = [32768, 33792]
A_RB = 34816
A_SUM = 36864

B_COLS = 14336
B_KOWN = [0, 1024]
B_VFM = [2048, 3072]
B_VTM = [4096, 5120]
NATT = 5
B_ATT = [6144 + 1024 * i for i in range(NATT)]
B_P = [11264, 12288, 13312]
B_XB = [0, 1024, 2048, 3072]
B_XSQ = [4096, 5120, 6144, 7168]
B_MEAN = 8192
B_ROW2 = 10240
B_RSTD = 12288


def _i_span(jt):
    lo = max(0, jt - 8)
    hi = min(8, jt + 1)
    return lo, hi


class Ctr:
    def __init__(self):
        self.v = 0

    def bump(self, k=1):
        self.v += k
        return self.v


class Rec:
    def __init__(self):
        self.ops = {k: [] for k in ("tensor", "scalar", "vector", "gpsimd", "sync")}

    def on(self, eng, fn):
        self.ops[eng].append(fn)


def build_nc():
    nc = bass.Bass("TRN2", target_bir_lowering=False, debug=False,
                   dynamic_dma_scratch_size=16384)

    x_d = nc.dram_tensor("x", [H, T], F32, kind="ExternalInput").ap()
    w0t_d = nc.dram_tensor("w0t", [H, 8192], F32, kind="ExternalInput").ap()
    w1t_d = nc.dram_tensor("w1t", [8192, H], F32, kind="ExternalInput").ap()
    wqt_d = nc.dram_tensor("wqt", [H, H], F32, kind="ExternalInput").ap()
    wkt_d = nc.dram_tensor("wkt", [H, H], F32, kind="ExternalInput").ap()
    wvt_d = nc.dram_tensor("wvt", [H, H], F32, kind="ExternalInput").ap()
    wot_d = nc.dram_tensor("wot", [H, H], F32, kind="ExternalInput").ap()
    b0_d = nc.dram_tensor("b0", [128, 64], F32, kind="ExternalInput").ap()
    b1_d = nc.dram_tensor("b1", [128, 16], F32, kind="ExternalInput").ap()
    maskl_d = nc.dram_tensor("maskl", [128, 128], F32, kind="ExternalInput").ap()
    maskd_d = nc.dram_tensor("maskd", [128, 128], F32, kind="ExternalInput").ap()
    maskz_d = nc.dram_tensor("maskz", [128, 128], F32, kind="ExternalInput").ap()
    idx_d = nc.dram_tensor("idx", [1, 1], I32, kind="ExternalInput").ap()
    y_d = nc.dram_tensor("y", [H, T], F32, kind="ExternalOutput").ap()

    wq_bf = nc.dram_tensor("wq_bf", [H, H], BF16)
    wk_bf = nc.dram_tensor("wk_bf", [H, H], BF16)
    wv_bf = nc.dram_tensor("wv_bf", [H, H], BF16)
    wo_bf = nc.dram_tensor("wo_bf", [H, H], BF16)
    kvc = [nc.dram_tensor(f"kvc{i}", [4, 128, 2048], BF16) for i in range(4)]
    kvg = [nc.dram_tensor(f"kvg{i}", [NCORES, 4, 128, 2048], BF16,
                          addr_space="Shared") for i in range(4)]
    qh_d = nc.dram_tensor("qh", [NH, 128, T], BF16)

    ident_h = nc.inline_tensor(np.eye(128, dtype=ml_dtypes.bfloat16), name="ident")
    ones128_h = nc.inline_tensor(np.ones((128, 1), dtype=ml_dtypes.bfloat16),
                                 name="ones128")
    onesrow_h = nc.inline_tensor(np.ones((1, 128), dtype=np.float32),
                                 name="onesrow")
    onesmat_h = nc.inline_tensor(np.ones((128, 128), dtype=ml_dtypes.bfloat16),
                                 name="onesmat")

    x_sb = nc.alloc_sbuf_tensor("x_sb", [128, HC * T], F32).ap()
    xln = nc.alloc_sbuf_tensor("xln", [128, HC * T], BF16).ap()
    A = nc.alloc_sbuf_tensor("arenaA", [128, A_COLS], BF16).ap()
    B = nc.alloc_sbuf_tensor("arenaB", [128, B_COLS], BF16).ap()
    b0_sb = nc.alloc_sbuf_tensor("b0_sb", [128, 64], F32).ap()
    b1_sb = nc.alloc_sbuf_tensor("b1_sb", [128, 16], F32).ap()
    b1h_sb = nc.alloc_sbuf_tensor("b1h_sb", [128, 16], F32).ap()
    maskl_sb = nc.alloc_sbuf_tensor("maskl_sb", [128, 128], BF16).ap()
    maskd_sb = nc.alloc_sbuf_tensor("maskd_sb", [128, 128], BF16).ap()
    maskz_sb = nc.alloc_sbuf_tensor("maskz_sb", [128, 128], BF16).ap()
    ident_sb = nc.alloc_sbuf_tensor("ident_sb", [128, 128], BF16).ap()
    ones_sb = nc.alloc_sbuf_tensor("ones_sb", [128, 1], BF16).ap()
    onesrow_sb = nc.alloc_sbuf_tensor("onesrow_sb", [1, 128], F32).ap()
    onesmat_sb = nc.alloc_sbuf_tensor("onesmat_sb", [128, 128], BF16).ap()
    eps_sb = nc.alloc_sbuf_tensor("eps_sb", [1, 1], F32).ap()
    idx_sb = nc.alloc_sbuf_tensor("idx_sb", [1, 1], I32).ap()
    masks = {"l": maskl_sb, "d": maskd_sb, "z": maskz_sb}

    Bf = B.bitcast(F32)
    Af = A.bitcast(F32)

    def b_f32(col_bf16, n_f32, p0=0, p1=128):
        return Bf[p0:p1, col_bf16 // 2: col_bf16 // 2 + n_f32]

    pA = nc.alloc_psum_tensor("pA", [128, 512], F32).ap()
    pB = nc.alloc_psum_tensor("pB", [128, 512], F32).ap()
    pC = nc.alloc_psum_tensor("pC", [128, 512], F32).ap()
    pD = nc.alloc_psum_tensor("pD", [128, 512], F32).ap()
    psbig = nc.alloc_psum_tensor("psbig", [128, 1024], F32).ap()
    psbig2 = nc.alloc_psum_tensor("psbig2", [128, 1024], F32).ap()
    pCD = [pC, pD]
    pCbf = pC.bitcast(BF16)

    sem_names = ["s_x", "s_xc", "s_w", "s_mm1", "s_gel", "s_mm2", "s_add",
                 "s_lnxb", "s_lnsq", "s_lnmm", "s_lnrow", "s_lnbc", "s_lnap",
                 "s_aw", "s_pe", "s_kvout", "s_cc", "s_win", "s_st", "s_tanh",
                 "s_exp", "s_sum", "s_pv", "s_rcp", "s_rb", "s_av", "s_oproj",
                 "s_oadd", "s_out", "s_init", "s_qw", "s_pc", "s_scp", "s_avc", "s_lnm"]
    sems, ctrs, sem_guards = {}, {}, []
    for n in sem_names:
        g = nc.semaphore(n)
        sems[n] = g.__enter__()
        sem_guards.append(g)
        ctrs[n] = Ctr()

    rec = Rec()

    def S(n):
        return sems[n]

    def C(n):
        return ctrs[n]

    def _sync_init(e):
        xr = x_d.rearrange("(c p) t -> p c t", p=128)
        for hc in range(HC):
            e.dma_start(out=x_sb[:, hc * T:(hc + 1) * T],
                        in_=xr[:, hc, :]).then_inc(S("s_xc"), 16)
        e.dma_start(out=b0_sb[:, :], in_=b0_d[:, :]).then_inc(S("s_x"), 16)
        e.dma_start(out=b1_sb[:, :], in_=b1_d[:, :]).then_inc(S("s_x"), 16)
        e.dma_start(out=ident_sb[:, :], in_=ident_h.ap()[:, :]).then_inc(S("s_x"), 16)
        e.dma_start(out=ones_sb[:, :], in_=ones128_h.ap()[:, :]).then_inc(S("s_x"), 16)
        e.dma_start(out=onesrow_sb[:, :], in_=onesrow_h.ap()[:, :]).then_inc(S("s_x"), 16)
        e.dma_start(out=onesmat_sb[:, :], in_=onesmat_h.ap()[:, :]).then_inc(S("s_x"), 16)
        e.dma_start(out=idx_sb[:, :], in_=idx_d[:, :]).then_inc(S("s_x"), 16)
    rec.on("sync", _sync_init)
    C("s_xc").bump(16 * HC)
    C("s_x").bump(16 * 7)

    def _gp_init(e):
        e.dma_start(out=masks["l"][:, :], in_=maskl_d[:, :]).then_inc(S("s_x"), 16)
        e.dma_start(out=masks["d"][:, :], in_=maskd_d[:, :]).then_inc(S("s_x"), 16)
        e.dma_start(out=masks["z"][:, :], in_=maskz_d[:, :]).then_inc(S("s_x"), 16)
    rec.on("gpsimd", _gp_init)
    C("s_x").bump(16 * 3)
    x_loaded = C("s_x").v

    def _vec_init(e):
        e.wait_ge(S("s_x"), x_loaded)
        e.memset(eps_sb[:, :], EPS)
        e.tensor_scalar_mul(b1h_sb[:, :], b1_sb[:, :], 0.5).then_inc(S("s_init"), 1)
    rec.on("vector", _vec_init)
    C("s_init").bump(1)
    init_done = C("s_init").v

    def emit_ln(ready_sem, ready_val, chunk_gate=None, pre_info=None,
                psum_wait=None):
        xb = [B[:, B_XB[p]:B_XB[p] + T] for p in range(4)]
        xsq = [B[:, B_XSQ[p]:B_XSQ[p] + T] for p in range(4)]
        mean_r = b_f32(B_MEAN, T, 0, 1)
        row2 = b_f32(B_ROW2, T, 0, 1)
        rstd_r = b_f32(B_RSTD, T, 0, 1)

        if pre_info is not None:
            lnmm_base = pre_info["lnmm"]
            lnsq_base = pre_info["lnsq"]
        else:
            lnmm_base = C("s_lnmm").v
            lnsq_base = C("s_lnsq").v
        lnxb_base = C("s_lnxb").v

        def _vec_cast(e):
            for hc in range(HC):
                if chunk_gate is not None:
                    gs, gv = chunk_gate(hc)
                    e.wait_ge(S(gs), gv)
                elif hc == 0:
                    e.wait_ge(S(ready_sem), ready_val)
                if hc >= 2:
                    e.wait_ge(S("s_lnmm"), lnmm_base + hc - 1)
                e.tensor_copy(xb[hc % 2][:, :],
                              x_sb[:, hc * T:(hc + 1) * T])
                e.tensor_mul(xsq[hc % 2][:, :], xb[hc % 2][:, :],
                             xb[hc % 2][:, :]).then_inc(S("s_lnsq"), 1)
        if pre_info is None:
            rec.on("vector", _vec_cast)
            C("s_lnsq").bump(HC)

        def _ten_stats(e):
            for hc in range(HC):
                if hc == 0:
                    e.wait_ge(S("s_x"), x_loaded)
                    if psum_wait is not None:
                        e.wait_ge(S(psum_wait[0]), psum_wait[1])
                e.wait_ge(S("s_lnsq"), lnsq_base + hc + 1)
                for ih in range(2):
                    e.matmul(psbig[:, ih * 512:(ih + 1) * 512], onesmat_sb[:, :],
                             xb[hc % 2][:, ih * 512:(ih + 1) * 512],
                             start=(hc == 0), stop=(hc == HC - 1))
                ins = None
                for ih in range(2):
                    ins = e.matmul(psbig2[:, ih * 512:(ih + 1) * 512],
                                   onesmat_sb[:, :],
                                   xsq[hc % 2][:, ih * 512:(ih + 1) * 512],
                                   start=(hc == 0), stop=(hc == HC - 1))
                ins.then_inc(S("s_lnmm"), 1)
        if pre_info is None:
            rec.on("tensor", _ten_stats)
            C("s_lnmm").bump(HC)
        stats_done = lnmm_base + HC

        def _vec_rows(e):
            e.wait_ge(S("s_lnmm"), stats_done)
            e.tensor_scalar_mul(mean_r, psbig[0:1, :], 1.0 / H).then_inc(S("s_lnm"), 1)
            e.tensor_scalar_mul(row2, psbig2[0:1, :], 1.0 / H)
            e.tensor_mul(rstd_r, mean_r, mean_r)
            e.tensor_sub(row2, row2, rstd_r).then_inc(S("s_lnrow"), 1)
        rec.on("vector", _vec_rows)
        C("s_lnm").bump(1)
        mean_done = C("s_lnm").v
        C("s_lnrow").bump(1)
        var_done = C("s_lnrow").v

        def _vec_pass1(e):
            e.wait_ge(S("s_lnmm"), stats_done)
            for hc in range(HC):
                e.scalar_tensor_tensor(
                    out=xln[:, hc * T:(hc + 1) * T], in0=psbig[:, :],
                    scalar=-1.0 / H, in1=x_sb[:, hc * T:(hc + 1) * T],
                    op0=ALU.mult, op1=ALU.add)
            e.engine_nop().then_inc(S("s_lnrow"), 1)
        rec.on("vector", _vec_pass1)
        C("s_lnrow").bump(1)

        def _sca_sqrt(e):
            e.wait_ge(S("s_lnrow"), var_done)
            e.activation(out=row2, in_=row2, func=AF.Sqrt,
                         bias=eps_sb[0:1, :]).then_inc(S("s_lnrow"), 1)
        rec.on("scalar", _sca_sqrt)
        C("s_lnrow").bump(1)
        sqrt_done = C("s_lnrow").v

        def _vec_recip(e):
            e.wait_ge(S("s_lnrow"), sqrt_done)
            e.reciprocal(rstd_r, row2).then_inc(S("s_lnrow"), 1)
        rec.on("vector", _vec_recip)
        C("s_lnrow").bump(1)
        rows_done = C("s_lnrow").v

        def _ten_bcR(e):
            e.wait_ge(S("s_lnrow"), rows_done)
            ins = None
            for ih in range(2):
                ins = e.matmul(psbig2[:, ih * 512:(ih + 1) * 512], onesrow_sb[:, :],
                               rstd_r[:, ih * 512:(ih + 1) * 512],
                               start=True, stop=True)
            ins.then_inc(S("s_lnbc"), 1)
        rec.on("tensor", _ten_bcR)
        C("s_lnbc").bump(1)
        bc_done = C("s_lnbc").v

        def _vec_apply(e):
            e.wait_ge(S("s_lnbc"), bc_done)
            for hc in range(HC):
                e.tensor_mul(xln[:, hc * T:(hc + 1) * T],
                             xln[:, hc * T:(hc + 1) * T],
                             psbig2[:, :]).then_inc(S("s_lnap"), 1)
        rec.on("vector", _vec_apply)
        C("s_lnap").bump(HC)
        return C("s_lnap").v

    def emit_ffn(ln_done, region_ready, gp_hook=None, early_gates=None,
                 ln_hooks=None):
        w0 = [A[:, A_W0[p]:A_W0[p] + 8192] for p in range(2)]
        w1 = [A[:, A_W1[p]:A_W1[p] + 8192] for p in range(2)]
        dt = [A[:, A_DT[p]:A_DT[p] + 4096] for p in range(2)]

        w_base = C("s_w").v
        mm1_base = C("s_mm1").v
        gel_base = C("s_gel").v
        mm2_base = C("s_mm2").v
        add_base = C("s_add").v

        def _vec_b1(e):
            e.wait_ge(S("s_init"), init_done)
            for hc in range(HC):
                e.tensor_scalar_add(x_sb[:, hc * T:(hc + 1) * T],
                                    x_sb[:, hc * T:(hc + 1) * T],
                                    b1h_sb[:, hc:hc + 1])
            e.engine_nop().then_inc(S("s_add"), 1)
        rec.on("vector", _vec_b1)
        C("s_add").bump(1)

        def _gp_w(e):
            if region_ready is not None:
                e.wait_ge(S(region_ready[0]), region_ready[1])
            for s in range(NSUP):
                if gp_hook is not None:
                    gp_hook(e, s)
                if s >= 2:
                    e.wait_ge(S("s_mm2"), mm2_base + 32 * (s - 1))
                if early_gates is not None and s in early_gates:
                    for gsem, gval in early_gates[s]:
                        e.wait_ge(S(gsem), gval)
                p = s % 2
                e.dma_start(
                    out=w0[p].rearrange("p (c n) -> p c n", n=512),
                    in_=w0t_d.rearrange("(c p) n -> p c n", p=128)[:, :, s * 512:(s + 1) * 512],
                ).then_inc(S("s_w"), 16)
                e.dma_start(
                    out=w1[p].rearrange("p (c h) -> p c h", h=H),
                    in_=w1t_d.rearrange("(S c p) h -> S p c h", p=128, c=4)[s],
                ).then_inc(S("s_w"), 16)
        rec.on("gpsimd", _gp_w)
        C("s_w").bump(32 * NSUP)

        def _ten(e):
            for s in range(NSUP):
                p = s % 2
                for nc4 in range(4):
                    for ih in range(2):
                        tile = s * 8 + nc4 * 2 + ih
                        ps = pA if tile % 2 == 0 else pB
                        if nc4 == 0 and ih == 0:
                            e.wait_ge(S("s_w"), w_base + 32 * (s + 1))
                        if tile >= 2:
                            e.wait_ge(S("s_gel"), gel_base + tile - 1)
                        ins = None
                        for hc in range(HC):
                            if s == 0 and nc4 == 0 and ih == 0:
                                e.wait_ge(S("s_lnap"), ln_done - HC + hc + 1)
                            ins = e.matmul(
                                ps[:, :],
                                w0[p][:, hc * 512 + nc4 * 128:hc * 512 + (nc4 + 1) * 128],
                                xln[:, hc * T + ih * 512:hc * T + (ih + 1) * 512],
                                start=(hc == 0), stop=(hc == HC - 1))
                        ins.then_inc(S("s_mm1"), 1)
                last = (s == NSUP - 1) and (ln_hooks is not None)
                for hcc in range(HC):
                    for ih2 in range(2):
                        et = s * 32 + hcc * 2 + ih2
                        if last:
                            psE = pCD[ih2]
                            prev_et = et - 4 if et < s * 32 + 2 else et - 2
                            e.wait_ge(S("s_add"), add_base + prev_et + 2)
                        else:
                            rot = et % 4
                            psE = (pC, pD, psbig[:, 0:512], psbig[:, 512:1024])[rot]
                            if et >= 4:
                                e.wait_ge(S("s_add"), add_base + et - 2)
                        if hcc == 0 and ih2 == 0:
                            e.wait_ge(S("s_gel"), gel_base + 8 * (s + 1))
                        ins = None
                        for nc4 in range(4):
                            ins = e.matmul(
                                psE,
                                w1[p][:, nc4 * 2048 + hcc * 128:nc4 * 2048 + (hcc + 1) * 128],
                                dt[p][:, nc4 * 1024 + ih2 * 512:nc4 * 1024 + (ih2 + 1) * 512],
                                start=(nc4 == 0), stop=(nc4 == 3))
                        ins.then_inc(S("s_mm2"), 1)
                        if last and ih2 == 1:
                            ln_hooks["ten"](e, hcc)
        rec.on("tensor", _ten)
        C("s_mm1").bump(8 * NSUP)
        C("s_mm2").bump(32 * NSUP)

        def _sca_gelu(e):
            for s in range(NSUP):
                p = s % 2
                for nc4 in range(4):
                    for ih in range(2):
                        tile = s * 8 + nc4 * 2 + ih
                        ps = pA if tile % 2 == 0 else pB
                        if s >= 2 and nc4 == 0 and ih == 0:
                            e.wait_ge(S("s_mm2"), mm2_base + 32 * (s - 1))
                        e.wait_ge(S("s_mm1"), mm1_base + tile + 1)
                        e.activation(
                            out=dt[p][:, nc4 * 1024 + ih * 512:nc4 * 1024 + (ih + 1) * 512],
                            in_=ps[:, :], func=AF.Gelu,
                            bias=b0_sb[:, s * 4 + nc4:s * 4 + nc4 + 1],
                        ).then_inc(S("s_gel"), 1)
        rec.on("scalar", _sca_gelu)
        C("s_gel").bump(8 * NSUP)

        def _vec_add(e):
            for s in range(NSUP):
                for hcc in range(HC):
                    for ih2 in range(2):
                        et = s * 32 + hcc * 2 + ih2
                        if (s == NSUP - 1) and (ln_hooks is not None):
                            psE = pCD[ih2]
                        else:
                            rot = et % 4
                            psE = (pC, pD, psbig[:, 0:512], psbig[:, 512:1024])[rot]
                        e.wait_ge(S("s_mm2"), mm2_base + et + 1)
                        xs = x_sb[:, hcc * T + ih2 * 512:hcc * T + (ih2 + 1) * 512]
                        e.scalar_tensor_tensor(
                            out=xs, in0=psE, scalar=0.5, in1=xs,
                            op0=ALU.mult, op1=ALU.add,
                        ).then_inc(S("s_add"), 1)
                        if (s == NSUP - 1) and (ln_hooks is not None) and ih2 == 1:
                            ln_hooks["vec"](e, hcc)
        rec.on("vector", _vec_add)
        C("s_add").bump(32 * NSUP)
        return C("s_add").v

    def make_ln_pre():
        xb = [B[:, B_XB[p]:B_XB[p] + T] for p in range(2)]
        xsq = [B[:, B_XSQ[p]:B_XSQ[p] + T] for p in range(2)]
        info = {"lnsq": C("s_lnsq").v, "lnmm": C("s_lnmm").v}

        def vec_hook(e, hc):
            e.tensor_copy(xb[hc % 2][:, :], x_sb[:, hc * T:(hc + 1) * T])
            e.tensor_mul(xsq[hc % 2][:, :], xb[hc % 2][:, :],
                         xb[hc % 2][:, :]).then_inc(S("s_lnsq"), 1)

        def ten_hook(e, hc):
            e.wait_ge(S("s_lnsq"), info["lnsq"] + hc + 1)
            for ih in range(2):
                e.matmul(psbig[0:1, ih * 512:(ih + 1) * 512], ones_sb[:, :],
                         xb[hc % 2][:, ih * 512:(ih + 1) * 512],
                         start=(hc == 0), stop=(hc == HC - 1))
            ins = None
            for ih in range(2):
                ins = e.matmul(psbig2[0:1, ih * 512:(ih + 1) * 512],
                               ones_sb[:, :],
                               xsq[hc % 2][:, ih * 512:(ih + 1) * 512],
                               start=(hc == 0), stop=(hc == HC - 1))
            ins.then_inc(S("s_lnmm"), 1)

        C("s_lnsq").bump(HC)
        C("s_lnmm").bump(HC)
        return {"ten": ten_hook, "vec": vec_hook}, info

    def emit_attn(ln_done, ffn1_add_done, ln_hooks=None):
        wq = [A[:, A_WQ[p]:A_WQ[p] + 2048] for p in range(2)]
        wk = [A[:, A_WK[p]:A_WK[p] + 2048] for p in range(2)]
        wv = [A[:, A_WV[p]:A_WV[p] + 2048] for p in range(2)]
        wo = [A[:, A_WO[p]:A_WO[p] + 2048] for p in range(4)]
        qown = [A[:, A_QOWN[p]:A_QOWN[p] + 1024] for p in range(2)]
        qst = [A[:, A_QST[p]:A_QST[p] + 1024] for p in range(2)]
        kst = [A[:, A_KST[p]:A_KST[p] + 2048] for p in range(2)]
        vst = [A[:, A_VST[p]:A_VST[p] + 2048] for p in range(2)]
        ts = [Af[:, A_TS[p] // 2: A_TS[p] // 2 + 512] for p in range(2)]
        kown = [B[:, B_KOWN[p]:B_KOWN[p] + 1024] for p in range(2)]
        vfm = [B[:, B_VFM[p]:B_VFM[p] + 1024] for p in range(2)]
        vtm = [B[:, B_VTM[p]:B_VTM[p] + 1024] for p in range(2)]
        att = [B[:, B_ATT[i]:B_ATT[i] + 1024] for i in range(NATT)]
        psl = [B[:, B_P[p]:B_P[p] + 1024] for p in range(3)]
        rb = Af[:, A_RB // 2: A_RB // 2 + 1024]
        sum_r = Af[0:1, A_SUM // 2: A_SUM // 2 + 1024]

        aw_base = C("s_aw").v
        qw_base = C("s_qw").v
        pe_base = C("s_pe").v
        st_base = C("s_st").v
        kvout_base = C("s_kvout").v
        cc_base = C("s_cc").v
        win_base = C("s_win").v
        tanh_base = C("s_tanh").v
        exp_base = C("s_exp").v
        sum_base = C("s_sum").v
        pv_base = C("s_pv").v
        rcp_base = C("s_rcp").v
        scp_base = C("s_scp").v
        avc_base = C("s_avc").v
        rb_base = C("s_rb").v
        av_base = C("s_av").v
        oproj_base = C("s_oproj").v
        oadd_base = C("s_oadd").v

        def pe_val(h, k):
            return pe_base + 7 * h + k + 1

        def st_val(h, k):
            return st_base + 7 * h + k + 1

        wqt_r = wq_bf.ap().rearrange("(c p) (G o) -> G p c o", p=128, o=128)
        wkt_r = wk_bf.ap().rearrange("(c p) (G o) -> G p c o", p=128, o=128)
        wvt_r = wv_bf.ap().rearrange("(c p) (G o) -> G p c o", p=128, o=128)
        wot_r = wo_bf.ap().rearrange("(G p) g -> G p g", p=128)

        def head_subs():
            half0 = [(jt, 0, max(0, jt - 8), min(4, jt + 1)) for jt in range(0, 12)]
            half1 = [(jt, 1, max(4, jt - 8), min(8, jt + 1)) for jt in range(4, 16)]
            out = []
            for a, b in zip(half0, half1):
                out.append(a)
                out.append(b)
            return out

        subs = []
        for h in range(NH):
            for (jt, half, l2, h2) in head_subs():
                subs.append((h, jt, half, l2, h2, jt == (0 if half == 0 else 4)))
        NSUBH = len(subs) // NH
        prev_parity = {}
        last_par = {}
        for si, (h, jt, half, l2, h2, stf) in enumerate(subs):
            key = (jt % 3, half)
            prev_parity[si] = last_par.get(key)
            last_par[key] = si

        def mask_ops(jt, l2, h2):
            out = []
            for ic in range(l2, h2):
                rt = jt - ic
                if rt == 0:
                    out.append((ic, "l"))
                elif rt == 8:
                    out.append((ic, "d"))
                elif jt < 8:
                    out.append((ic, "z"))
            return out

        def _sync_attn1(e):
            def slices(h):
                p = h % 2
                if h >= 2:
                    e.wait_ge(S("s_st"), st_val(h - 2, 6))
                e.dma_start(out=wq[p].rearrange("p (c o) -> p c o", o=128),
                            in_=wqt_r[h]).then_inc(S("s_aw"), 16)
                e.dma_start(out=wk[p].rearrange("p (c o) -> p c o", o=128),
                            in_=wkt_r[h]).then_inc(S("s_aw"), 16)
                e.dma_start(out=wv[p].rearrange("p (c o) -> p c o", o=128),
                            in_=wvt_r[h]).then_inc(S("s_aw"), 16)

            e.wait_ge(S("s_add"), ffn1_add_done)
            e.wait_ge(S("s_pc"), 256)
            e.wait_ge(S("s_x"), x_loaded)
            prev = e.value_load(idx_sb[0:1, 0:1])

            def kv(h):
                p = h % 2
                g, hh = divmod(h, 4)
                e.wait_ge(S("s_pe"), pe_val(h, 3))
                e.dma_start(out=qh_d.ap()[h], in_=qown[p][:, :]).then_inc(S("s_kvout"), 16)
                e.dma_start(out=kvc[g].ap()[hh, :, 0:1024],
                            in_=kown[p][:, :]).then_inc(S("s_kvout"), 16)
                e.wait_ge(S("s_pe"), pe_val(h, 6))
                e.dma_start(out=kvc[g].ap()[hh, :, 1024:2048],
                            in_=vtm[p][:, :]).then_inc(S("s_kvout"), 16)

            def wowin(h):
                p = h % 2
                g, hh = divmod(h, 4)
                if h >= 4:
                    e.wait_ge(S("s_oproj"),
                              oproj_base + 32 * ((h - 4) // 4 + 1))
                e.dma_start(out=wo[h % 4][:, :],
                            in_=wot_r[h]).then_inc(S("s_qw"), 16)
                e.wait_ge(S("s_cc"), cc_base + g + 1)
                if h >= 2:
                    e.wait_ge(S("s_pv"), pv_base + NSUBH * (h - 1))
                kv_halo = kvg[g].ap()[bass.ts(prev, 1)].squeeze(0)
                e.dma_start(out=kst[p][:, 0:1024],
                            in_=kv_halo[hh, :, 0:1024]).then_inc(S("s_win"), 16)
                e.dma_start(out=vst[p][:, 0:1024],
                            in_=kv_halo[hh, :, 1024:2048]).then_inc(S("s_win"), 16)
                e.dma_start(out=kst[p][:, 1024:2048],
                            in_=kvc[g].ap()[hh, :, 0:1024]).then_inc(S("s_win"), 16)
                e.dma_start(out=vst[p][:, 1024:2048],
                            in_=kvc[g].ap()[hh, :, 1024:2048]).then_inc(S("s_win"), 16)
                e.dma_start(out=qst[p][:, :],
                            in_=qh_d.ap()[h]).then_inc(S("s_win"), 16)

            slices(0)
            slices(1)
            for h in range(NH):
                if h + 2 < NH:
                    slices(h + 2)
                kv(h)
            for h in range(NH):
                wowin(h)
        rec.on("sync", _sync_attn1)
        C("s_aw").bump(48 * NH)
        C("s_kvout").bump(48 * NH)

        def _gp_cc(e):
            for g in range(4):
                e.wait_ge(S("s_kvout"), kvout_base + 48 * 4 * (g + 1))
                e.collective_compute(
                    "AllGather", ALU.bypass,
                    replica_groups=[list(range(NCORES))],
                    ins=[kvc[g].ap().opt()],
                    outs=[kvg[g].ap().opt()],
                ).then_inc(S("s_cc"), 1)
        rec.on("gpsimd", _gp_cc)
        C("s_cc").bump(4)

        C("s_qw").bump(16 * NH)
        C("s_win").bump(80 * NH)

        def _ten_proj(e):
            for h in range(NH):
                p = h % 2
                e.wait_ge(S("s_aw"), aw_base + 48 * (h + 1))
                for wi, wsl in ((0, wq[p]), (1, wk[p]), (2, wv[p])):
                    for th in range(2):
                        t6 = h * 6 + wi * 2 + th
                        ps = pCD[t6 % 2]
                        if t6 >= 2:
                            pt = t6 - 2
                            e.wait_ge(S("s_pe"), pe_val(pt // 6, pt % 6))
                        if wi == 0 and th == 0 and h >= 1:
                            e.wait_ge(S("s_pe"), pe_val(h - 1, 6))
                        ins = None
                        for hc in range(HC):
                            if t6 == 0:
                                e.wait_ge(S("s_lnap"), ln_done - HC + hc + 1)
                            ins = e.matmul(
                                ps[:, :],
                                wsl[:, hc * 128:(hc + 1) * 128],
                                xln[:, hc * T + th * 512:hc * T + (th + 1) * 512],
                                start=(hc == 0), stop=(hc == HC - 1))
                        ins.then_inc(S("s_st"), 1)
                e.wait_ge(S("s_pe"), pe_val(h, 5))
                ins = None
                for tc in range(8):
                    ins = e.transpose(pCbf[:, tc * 128:(tc + 1) * 128],
                                      vfm[p][:, tc * 128:(tc + 1) * 128],
                                      ident_sb[:, :])
                ins.then_inc(S("s_st"), 1)
        rec.on("tensor", _ten_proj)
        C("s_st").bump(7 * NH)

        def _sca_pe(e):
            for h in range(NH):
                p = h % 2
                for wi, dst in ((0, qown[p]), (1, kown[p]), (2, vfm[p])):
                    for th in range(2):
                        t6 = h * 6 + wi * 2 + th
                        e.wait_ge(S("s_st"), st_val(h, wi * 2 + th))
                        if wi == 0 and th == 0 and h >= 2:
                            e.wait_ge(S("s_kvout"), kvout_base + 48 * (h - 1))
                        e.copy(out=dst[:, th * 512:(th + 1) * 512],
                               in_=pCD[t6 % 2][:, :]).then_inc(S("s_pe"), 1)
        rec.on("scalar", _sca_pe)
        C("s_pe").bump(6 * NH)

        def _vec_vtm(e):
            for h in range(NH):
                p = h % 2
                e.wait_ge(S("s_st"), st_val(h, 6))
                if h >= 2:
                    e.wait_ge(S("s_kvout"), kvout_base + 48 * (h - 1))
                e.tensor_copy(vtm[p][:, :], pCbf[:, :]).then_inc(S("s_pe"), 1)
        rec.on("vector", _vec_vtm)
        C("s_pe").bump(NH)

        def _ten_scores(e):
            def do_st(si):
                h, jt, half, l2, h2, stf = subs[si]
                p = h % 2
                ps = pA if si % 2 == 0 else pB
                span = (h2 - l2) * 128
                k = si % NSUBH
                if k == 0:
                    e.wait_ge(S("s_win"), win_base + 80 * (h + 1))
                if si >= 2:
                    e.wait_ge(S("s_tanh"), tanh_base + si - 1)
                mo = mask_ops(jt, l2, h2)
                ins = e.matmul(ps[:, 0:span], kst[p][:, jt * 128:(jt + 1) * 128],
                               qst[p][:, l2 * 128:h2 * 128],
                               start=True, stop=(not mo))
                for mi, (ic, mk) in enumerate(mo):
                    r0 = (ic - l2) * 128
                    ins = e.matmul(ps[:, r0:r0 + 128], ident_sb[:, :],
                                   masks[mk][:, :], start=False,
                                   stop=(mi == len(mo) - 1))
                ins.then_inc(S("s_st"), 1)

            def do_sumpv(si):
                h, jt, half, l2, h2, stf = subs[si]
                p = h % 2
                k = si % NSUBH
                e.wait_ge(S("s_exp"), exp_base + si + 1)
                if k == 0 and h > 0:
                    e.wait_ge(S("s_rcp"), rcp_base + h)
                e.matmul(psbig2[:, l2 * 128:h2 * 128], onesmat_sb[:, :],
                         psl[jt % 3][:, l2 * 128:h2 * 128],
                         start=stf, stop=False).then_inc(S("s_sum"), 1)
                if k == 0 and h > 0:
                    e.wait_ge(S("s_avc"), avc_base + h)
                e.matmul(psbig[:, l2 * 128:h2 * 128],
                         vst[p][:, jt * 128:(jt + 1) * 128],
                         psl[jt % 3][:, l2 * 128:h2 * 128],
                         start=stf, stop=False).then_inc(S("s_pv"), 1)

            def do_oproj(g):
                e.wait_ge(S("s_av"), av_base + 4 * (g + 1))
                e.wait_ge(S("s_qw"), qw_base + 64 * (g + 1))
                if g == 0:
                    e.wait_ge(S("s_pe"), pe_val(NH - 1, 6))
                for g2 in range(HC):
                    for th in range(2):
                        ot = g * 32 + g2 * 2 + th
                        pso = pCD[ot % 2]
                        if ot >= 2:
                            e.wait_ge(S("s_oadd"), oadd_base + ot - 1)
                        elif g > 0:
                            e.wait_ge(S("s_oadd"), oadd_base + 32 * g)
                        ins = None
                        for i in range(4):
                            hh = 4 * g + i
                            ins = e.matmul(
                                pso[:, :],
                                wo[hh % 4][:, g2 * 128:(g2 + 1) * 128],
                                att[hh % NATT][:, th * 512:(th + 1) * 512],
                                start=(i == 0), stop=(i == 3))
                        ins.then_inc(S("s_oproj"), 1)
                        if g == 3 and th == 1 and ln_hooks is not None:
                            ln_hooks["ten"](e, g2)

            DELAY = 3
            for h in range(NH):
                for k in range(NSUBH):
                    si = h * NSUBH + k
                    do_st(si)
                    if k >= DELAY:
                        do_sumpv(si - DELAY)
                for k in range(NSUBH - DELAY, NSUBH):
                    do_sumpv(h * NSUBH + k)
                if h % 4 == 3:
                    do_oproj(h // 4)
        rec.on("tensor", _ten_scores)
        C("s_st").bump(len(subs))
        C("s_sum").bump(len(subs))
        C("s_pv").bump(len(subs))
        C("s_oproj").bump(128)

        def _sca_scores(e):
            for si, (h, jt, half, l2, h2, stf) in enumerate(subs):
                pq = si % 2
                ps = pA if pq == 0 else pB
                span = (h2 - l2) * 128
                e.wait_ge(S("s_st"), st_base + 7 * NH + si + 1)
                e.activation(out=ts[pq][:, 0:span], in_=ps[:, 0:span],
                             func=AF.Tanh, scale=float(SCALE / CAP)
                             ).then_inc(S("s_tanh"), 1)
                pp = prev_parity[si]
                if pp is not None:
                    e.wait_ge(S("s_pv"), pv_base + pp + 1)
                e.activation(out=psl[jt % 3][:, l2 * 128:h2 * 128],
                             in_=ts[pq][:, 0:span],
                             func=AF.Exp, scale=float(CAP)).then_inc(S("s_exp"), 1)
                if si % NSUBH == NSUBH - 1:
                    e.wait_ge(S("s_pv"), pv_base + NSUBH * (h + 1))
                    if h >= NATT:
                        e.wait_ge(S("s_oproj"),
                                  oproj_base + 32 * ((h - NATT) // 4 + 1))
                    e.copy(out=att[h % NATT][:, :],
                           in_=psbig[:, :]).then_inc(S("s_avc"), 1)
        rec.on("scalar", _sca_scores)
        C("s_tanh").bump(len(subs))
        C("s_exp").bump(len(subs))
        C("s_avc").bump(NH)

        def _vec_scores(e):
            for si, (h, jt, half, l2, h2, stf) in enumerate(subs):
                if si % NSUBH == NSUBH - 1:
                    e.wait_ge(S("s_sum"), sum_base + NSUBH * (h + 1))
                    e.tensor_copy(rb[:, 0:512], psbig2[:, 0:512])
                    e.tensor_copy(rb[:, 512:1024],
                                  psbig2[:, 512:1024]).then_inc(S("s_rcp"), 1)
                    e.reciprocal(rb, rb)
                    e.wait_ge(S("s_avc"), avc_base + h + 1)
                    e.scalar_tensor_tensor(
                        out=att[h % NATT][:, :], in0=att[h % NATT][:, :],
                        scalar=1.0, in1=rb, op0=ALU.mult, op1=ALU.mult,
                    ).then_inc(S("s_av"), 1)
                    if h % 4 == 3:
                        g = h // 4
                        for g2 in range(HC):
                            for th in range(2):
                                ot = g * 32 + g2 * 2 + th
                                pso = pCD[ot % 2]
                                e.wait_ge(S("s_oproj"), oproj_base + ot + 1)
                                xs = x_sb[:, g2 * T + th * 512:g2 * T + (th + 1) * 512]
                                e.tensor_add(xs, xs, pso[:, :]).then_inc(S("s_oadd"), 1)
                                if g == 3 and th == 1 and ln_hooks is not None:
                                    ln_hooks["vec"](e, g2)
        rec.on("vector", _vec_scores)
        C("s_rcp").bump(NH)
        C("s_av").bump(NH)
        C("s_oadd").bump(128)
        return {
            "oadd": C("s_oadd").v,
            "proj_st": st_base + 7 * NH,
            "oproj": oproj_base + 128,
            "kvout": kvout_base + 48 * NH,
            "win": win_base + 80 * NH,
            "pv": pv_base + NSUBH * NH,
        }

    ln1_done = emit_ln("s_xc", 16 * HC,
                       chunk_gate=lambda hc: ("s_xc", 16 * (hc + 1)))
    _pc_pairs = [(dst, src) for dst, src in
                 ((wq_bf, wqt_d), (wk_bf, wkt_d), (wv_bf, wvt_d), (wo_bf, wot_d))]

    def _precast(e, sidx):
        for k in range(2):
            i = (sidx - 2) * 2 + k
            if 0 <= i < 16:
                dst, src = _pc_pairs[i // 4]
                r0 = (i % 4) * 512
                e.dma_start(out=dst.ap()[r0:r0 + 512, :],
                            in_=src[r0:r0 + 512, :]).then_inc(S("s_pc"), 16)
    C("s_pc").bump(256)

    ffn1_done = emit_ffn(ln1_done, None, gp_hook=_precast)
    ln2_done = emit_ln("s_add", ffn1_done, psum_wait=("s_add", ffn1_done),
                       chunk_gate=lambda hc: ("s_add", ffn1_done - 30 + 2 * hc))
    ainfo = emit_attn(ln2_done, ffn1_done)
    attn_done = ainfo["oadd"]
    ln3_done = emit_ln("s_oadd", attn_done, psum_wait=("s_oadd", attn_done),
                       chunk_gate=lambda hc: ("s_oadd", attn_done - 30 + 2 * hc))
    ffn2_done = emit_ffn(ln3_done, None, early_gates={
        0: [("s_st", ainfo["proj_st"]), ("s_oproj", ainfo["oproj"]),
            ("s_kvout", ainfo["kvout"]), ("s_pv", ainfo["pv"])],
        1: [("s_pv", ainfo["pv"]), ("s_win", ainfo["win"]),
            ("s_oproj", ainfo["oproj"])],
    })

    def _sync_out(e):
        e.wait_ge(S("s_add"), ffn2_done)
        e.dma_start(out=y_d.rearrange("(c p) t -> p c t", p=128),
                    in_=x_sb.rearrange("p (c t) -> p c t", t=T)
                    ).then_inc(S("s_out"), 16)
        e.wait_ge(S("s_out"), 16)
    rec.on("sync", _sync_out)
    C("s_out").bump(16)

    with nc.Block() as block:
        @block.tensor
        def _(eng):
            for fn in rec.ops["tensor"]:
                fn(eng)

        @block.scalar
        def _(eng):
            for fn in rec.ops["scalar"]:
                fn(eng)

        @block.vector
        def _(eng):
            for fn in rec.ops["vector"]:
                fn(eng)

        @block.gpsimd
        def _(eng):
            for fn in rec.ops["gpsimd"]:
                fn(eng)

        @block.sync
        def _(eng):
            for fn in rec.ops["sync"]:
                fn(eng)

    for g in reversed(sem_guards):
        g.__exit__(None, None, None)
    return nc


_NC_CACHE = None


def _get_nc():
    global _NC_CACHE
    if _NC_CACHE is None:
        _NC_CACHE = build_nc()
    return _NC_CACHE


def kernel(L0_attn_x, L0_attn_Wq, L0_attn_Wk, L0_attn_Wv, L0_attn_Wo,
           L0_ffn_W0, L0_ffn_bias0, L0_ffn_W1, L0_ffn_bias1):
    x = np.asarray(L0_attn_x)
    w0t = np.ascontiguousarray(np.asarray(L0_ffn_W0).T)
    w1t = np.ascontiguousarray(np.asarray(L0_ffn_W1).T)
    wqt = np.ascontiguousarray(np.asarray(L0_attn_Wq).T)
    wkt = np.ascontiguousarray(np.asarray(L0_attn_Wk).T)
    wvt = np.ascontiguousarray(np.asarray(L0_attn_Wv).T)
    wot = np.ascontiguousarray(np.asarray(L0_attn_Wo).T)
    b0_l = np.ascontiguousarray(np.asarray(L0_ffn_bias0).reshape(64, 128).T)
    b1_l = np.ascontiguousarray(np.asarray(L0_ffn_bias1).reshape(16, 128).T)

    u = np.arange(128)[:, None]
    v = np.arange(128)[None, :]
    NEG = np.float32(-100000.0)
    m_lower = np.where(u > v, 0.0, NEG).astype(np.float32)
    m_diag = np.where(u <= v, 0.0, NEG).astype(np.float32)
    m_ones = np.zeros((128, 128), np.float32)
    m_zero = np.full((128, 128), NEG, np.float32)

    nc = _get_nc()
    in_maps = []
    for c in range(NCORES):
        xc = np.ascontiguousarray(x[0, c * T:(c + 1) * T, :].T)
        in_maps.append({
            "x": xc, "w0t": w0t, "w1t": w1t,
            "wqt": wqt, "wkt": wkt, "wvt": wvt, "wot": wot,
            "b0": b0_l, "b1": b1_l,
            "maskl": (m_zero if c == 0 else m_lower),
            "maskd": m_diag,
            "maskz": (m_zero if c == 0 else m_ones),
            "idx": np.array([[(c - 1) % NCORES]], np.int32),
        })
    res = run_bass_kernel_spmd(nc, in_maps, core_ids=list(range(NCORES)))
    out = np.empty((1, NCORES * T, H), np.float32)
    for c in range(NCORES):
        out[0, c * T:(c + 1) * T, :] = res.results[c]["y"].T
    return out
